# version 04
# baseline (speedup 1.0000x reference)
from contextlib import ExitStack

import numpy as np
import ml_dtypes
import concourse.bass as bass
import concourse.mybir as mybir
from concourse.bass_utils import run_bass_kernel_spmd
from concourse.bass_core_rust import IndirectOffsetOnAxis

F32 = mybir.dt.float32
BF16 = mybir.dt.bfloat16
I32 = mybir.dt.int32
AF = mybir.ActivationFunctionType
ALU = mybir.AluOpType
AX = mybir.AxisListType
bfd = ml_dtypes.bfloat16

NCORES = 8
T = 2048
NCH = 16
D = 1024
DC = 8
H0 = 3072
HM = 4096
CAP = 2560
NSB = CAP // 128
HEADS = 16
EPS = 1e-5
QDEQ = float(np.float32(1.0) / np.float32(127.0))
TALL = NCORES * T
TCH = TALL // 128
VW = 65
SENT = TALL
ODUMP = NCORES * CAP


class Ctr:
    def __init__(self, stack, nc, name):
        self.h = stack.enter_context(nc.semaphore(name))
        self.n = 0

    def inc(self, inst, n=1):
        inst.then_inc(self.h, n)
        self.n += n
        return inst

    def tok(self):
        return (("c", self.h, self.n),)

    def wait(self, eng, tok):
        for kind, h, v in tok:
            if v > 0:
                eng.wait_ge(h, v)


class DmaQ:

    NSEM = 8

    def __init__(self, stack, nc, name):
        self.sems = [stack.enter_context(nc.semaphore(f"{name}{i}"))
                     for i in range(self.NSEM)]
        self.cnt = [0] * self.NSEM
        self.cur = 0

    def inc(self, inst, n=16):
        i = self.cur
        self.cur = (i + 1) % self.NSEM
        if self.cnt[i] > 0:
            inst._wait_ge(self.sems[i], self.cnt[i])
        inst.then_inc(self.sems[i], n)
        self.cnt[i] += n
        return inst

    def tok(self):
        return tuple(("q", self.sems[i], self.cnt[i]) for i in range(self.NSEM)
                     if self.cnt[i] > 0)

    def wait(self, eng, tok):
        for kind, h, v in tok:
            eng.wait_ge(h, v)


class P:
    def __init__(self, nc, block, stack):
        self.nc = nc
        self.block = block
        self.stack = stack
        self.dw = DmaQ(stack, nc, "s_dw")
        self.dg = DmaQ(stack, nc, "s_dg")
        self.pe = Ctr(stack, nc, "s_pe")
        self.act = Ctr(stack, nc, "s_act")
        self.dve = Ctr(stack, nc, "s_dve")

    def fence(self):
        toks = self.dw.tok() + self.dg.tok() + self.pe.tok() + self.act.tok() + self.dve.tok()
        own = {"tensor": self.pe.h, "vector": self.dve.h, "scalar": self.act.h,
               "gpsimd": None, "sync": None}
        for ename, sec in (("tensor", self.block.tensor), ("vector", self.block.vector),
                           ("scalar", self.block.scalar), ("gpsimd", self.block.gpsimd),
                           ("sync", self.block.sync)):
            skip = own[ename]

            def f(e, toks=toks, skip=skip):
                for kind, h, v in toks:
                    if h is not skip and v > 0:
                        e.wait_ge(h, v)
            sec(f)


def load_consts(p, C, keys):
    nc = p.nc
    for k in keys:
        d = C[k]
        t = p.stack.enter_context(
            nc.sbuf_tensor("cs_" + k, list(d.shape), d.dtype))

        def f(sy, t=t, d=d):
            p.dw.inc(sy.dma_start(out=t[:, :], in_=d[:, :]), 16)
        p.block.sync(f)
        C[k] = t[:, :]
    p.fence()


def make_consts(nc):
    c = {}
    c["id128"] = nc.inline_tensor(np.eye(128, dtype=bfd), "c_id128").ap()
    c["id128f"] = nc.inline_tensor(np.eye(128, dtype=np.float32), "c_id128f").ap()
    c["ones_bf"] = nc.inline_tensor(np.ones((128, 128), dtype=bfd), "c_ones_bf").ap()
    c["ones_f32"] = nc.inline_tensor(np.ones((1, 128), dtype=np.float32), "c_ones_f32").ap()
    qs = np.ones((128, 1), np.float32)
    qs[:64] = 0.125
    c["qkscale"] = nc.inline_tensor(qs, "c_qkscale").ap()
    half = 32
    theta = 1.0 / 10000.0 ** (np.arange(half, dtype=np.float32) / half)
    ang = np.arange(T, dtype=np.float32)[None, :] * theta[:, None]
    c["ropecos"] = nc.inline_tensor(np.cos(ang).astype(bfd), "c_ropecos").ap()
    c["ropesin"] = nc.inline_tensor(np.sin(ang).astype(bfd), "c_ropesin").ap()
    tid = (np.arange(128)[:, None] * 128 + np.arange(128)[None, :]).astype(np.int32)
    tid2 = np.repeat(tid[:, :, None], 2, axis=2).reshape(128, 256)
    c["tid2"] = nc.inline_tensor(tid2, "c_tid2").ap()
    c["iota_e"] = nc.inline_tensor(
        np.broadcast_to(np.arange(8, dtype=np.float32), (128, NCH, 8)).reshape(128, NCH * 8).copy(),
        "c_iota_e").ap()
    return c



def emit_ln(p, C, Xsb, lnxfm, c0, c1, lnxtm=None):
    nc = p.nc
    ntok = (c1 - c0) * 128
    with (
        nc.sbuf_tensor(None, [128, 4 * NCH], F32, allow_name_mangling=True) as st,
        nc.sbuf_tensor(None, [128, 1024], F32, allow_name_mangling=True) as sq,
        nc.sbuf_tensor(None, [128, 2 * 1024], BF16, allow_name_mangling=True) as tmbuf,
        nc.psum_tensor(None, [128, 2048], BF16, allow_name_mangling=True) as ptr,
    ):
        sums = st[:, 0:NCH]
        sqs = st[:, NCH:2 * NCH]
        rstd = st[:, 2 * NCH:3 * NCH]
        nbias = st[:, 3 * NCH:4 * NCH]
        sl = slice(c0, c1)

        def f_sum(v):
            for c in range(c0, c1):
                p.dve.inc(v.tensor_reduce(sums[:, c:c + 1], Xsb[:, c * 1024:(c + 1) * 1024], AX.X, ALU.add))
        p.block.vector(f_sum)

        def f_sq(s):
            for c in range(c0, c1):
                p.act.inc(s.activation(sq[:, :], Xsb[:, c * 1024:(c + 1) * 1024], AF.Square,
                             accum_out=sqs[:, c:c + 1]))
        p.block.scalar(f_sq)
        a0 = p.act.n

        def f_stats(v):
            v.wait_ge(p.act.h, a0)
            v.wait_ge(p.dve.h, p.dve.n)
            p.dve.inc(v.tensor_scalar_mul(sums[:, sl], sums[:, sl], 1.0 / 1024.0))
            v.wait_ge(p.dve.h, p.dve.n)
            p.dve.inc(v.tensor_scalar_mul(sqs[:, sl], sqs[:, sl], 1.0 / 1024.0))
            v.wait_ge(p.dve.h, p.dve.n)
            p.dve.inc(v.scalar_tensor_tensor(rstd[:, sl], sums[:, sl], -1.0, sums[:, sl], ALU.mult, ALU.mult))
            v.wait_ge(p.dve.h, p.dve.n)
            p.dve.inc(v.tensor_add(rstd[:, sl], rstd[:, sl], sqs[:, sl]))
            v.wait_ge(p.dve.h, p.dve.n)
            p.dve.inc(v.tensor_scalar_add(rstd[:, sl], rstd[:, sl], EPS))
        p.block.vector(f_stats)
        v1 = p.dve.n

        def f_rstd(s):
            s.wait_ge(p.dve.h, v1)
            s.activation(rstd[:, sl], rstd[:, sl], AF.Ln)
            p.act.inc(s.activation(rstd[:, sl], rstd[:, sl], AF.Exp, scale=-0.5))
        p.block.scalar(f_rstd)
        a1 = p.act.n

        def f_nb(v):
            v.wait_ge(p.act.h, a1)
            p.dve.inc(v.tensor_scalar_mul(nbias[:, sl], sums[:, sl], -1.0))
        p.block.vector(f_nb)
        v2 = p.dve.n

        pe_marks, dve_marks = {}, {}
        for c in range(c0, c1):
            ci = c - c0
            par = ci % 2
            tmv = lnxtm[:, ci * 1024:(ci + 1) * 1024] if lnxtm is not None \
                else tmbuf[:, par * 1024:(par + 1) * 1024]

            def f_norm(v, c=c, ci=ci, tmv=tmv):
                if ci == 0:
                    v.wait_ge(p.dve.h, v2)
                if lnxtm is None and ci >= 2:
                    v.wait_ge(p.pe.h, pe_marks[ci - 2])
                p.dve.inc(v.tensor_scalar(tmv, Xsb[:, c * 1024:(c + 1) * 1024],
                                          nbias[:, c:c + 1], rstd[:, c:c + 1],
                                          ALU.add, ALU.mult))
            p.block.vector(f_norm)
            an = p.dve.n
            ptv = ptr[:, par * 1024:(par + 1) * 1024]

            def f_tr(t, ci=ci, tmv=tmv, ptv=ptv, an=an):
                t.wait_ge(p.dve.h, an)
                if ci >= 2:
                    t.wait_ge(p.dve.h, dve_marks[ci - 2])
                for dc in range(DC):
                    p.pe.inc(t.transpose(ptv[:, dc * 128:(dc + 1) * 128],
                                tmv[:, dc * 128:(dc + 1) * 128], C["id128"][:, :]))
            p.block.tensor(f_tr)
            pe_marks[ci] = p.pe.n
            pn = p.pe.n

            def f_ev(v, ci=ci, ptv=ptv, pn=pn):
                v.wait_ge(p.pe.h, pn)
                for dc in range(DC):
                    p.dve.inc(v.tensor_copy(lnxfm[:, dc * ntok + ci * 128: dc * ntok + (ci + 1) * 128],
                                  ptv[:, dc * 128:(dc + 1) * 128]))
            p.block.vector(f_ev)
            dve_marks[ci] = p.dve.n
    p.fence()



def emit_ffn0(p, C, Xsb, wg_t, wu_t, wd_t):
    nc = p.nc
    TG = 1024
    HG = 512
    for tg in range(T // TG):
        c0, c1 = tg * (TG // 128), (tg + 1) * (TG // 128)
        with (
            nc.sbuf_tensor(None, [128, (H0 // 128) * TG], BF16, allow_name_mangling=True) as hsb,
            nc.sbuf_tensor(None, [128, 2 * 512], F32, allow_name_mangling=True) as sgb,
        ):
            with (
                nc.sbuf_tensor(None, [128, DC * TG], BF16, allow_name_mangling=True) as lnxfm,
                nc.sbuf_tensor(None, [128, 4 * DC * HG], BF16, allow_name_mangling=True) as wbuf,
            ):
                emit_ln(p, C, Xsb, lnxfm, c0, c1)
                with nc.psum_tensor(None, [128, 2048], F32, allow_name_mangling=True) as ps:
                    pe_marks, ev_marks = {}, {}
                    for hg in range(H0 // HG):
                        par = hg % 2
                        wgv = wbuf[:, (2 * par) * DC * HG:(2 * par + 1) * DC * HG]
                        wuv = wbuf[:, (2 * par + 1) * DC * HG:(2 * par + 2) * DC * HG]
                        prev = pe_marks.get(hg - 2)

                        def f_w(g, hg=hg, wgv=wgv, wuv=wuv, prev=prev):
                            if prev is not None:
                                g.wait_ge(p.pe.h, prev)
                            for w_t, dst in ((wg_t, wgv), (wu_t, wuv)):
                                src = w_t.rearrange("(dc q) h -> q dc h", q=128)[:, :, hg * HG:(hg + 1) * HG]
                                p.dg.inc(g.dma_start(
                                    out=dst.rearrange("q (dc h) -> q dc h", dc=DC), in_=src), 16)
                        p.block.gpsimd(f_w)
                        wn = p.dg.tok()

                        for hc in range(HG // 128):
                            hcabs = hg * (HG // 128) + hc
                            for tt in range(TG // 512):
                                i = hcabs * (TG // 512) + tt
                                sub = i % 2
                                pg = ps[:, sub * 1024: sub * 1024 + 512]
                                pu = ps[:, sub * 1024 + 512: sub * 1024 + 1024]
                                emark = ev_marks.get(i - 2)

                                def f_mm(t, wgv=wgv, wuv=wuv, hc=hc, tt=tt, wn=wn,
                                         pg=pg, pu=pu, emark=emark):
                                    p.dg.wait(t, wn)
                                    if emark is not None:
                                        t.wait_ge(p.dve.h, emark)
                                    for wv_, pp in ((wgv, pg), (wuv, pu)):
                                        wr = wv_.rearrange("q (dc h) -> q dc h", dc=DC)
                                        for dc in range(DC):
                                            p.pe.inc(t.matmul(pp, wr[:, dc, hc * 128:(hc + 1) * 128],
                                                     lnxfm[:, dc * TG + tt * 512: dc * TG + (tt + 1) * 512],
                                                     start=(dc == 0), stop=(dc == DC - 1)))
                                p.block.tensor(f_mm)
                                mmn = p.pe.n
                                sgv = sgb[:, (i % 2) * 512:(i % 2 + 1) * 512]

                                def f_silu(s, pg=pg, mmn=mmn, sgv=sgv, emark=emark):
                                    s.wait_ge(p.pe.h, mmn)
                                    if emark is not None:
                                        s.wait_ge(p.dve.h, emark)
                                    p.act.inc(s.activation(sgv, pg, AF.Silu, scale=QDEQ))
                                p.block.scalar(f_silu)
                                sn = p.act.n
                                hv = hsb[:, hcabs * TG + tt * 512: hcabs * TG + (tt + 1) * 512]

                                def f_h(v, pu=pu, sgv=sgv, hv=hv, sn=sn):
                                    v.wait_ge(p.act.h, sn)
                                    p.dve.inc(v.scalar_tensor_tensor(hv, pu, QDEQ, sgv, ALU.mult, ALU.mult))
                                p.block.vector(f_h)
                                ev_marks[i] = p.dve.n
                        pe_marks[hg] = p.pe.n
                p.fence()

            with (
                nc.sbuf_tensor(None, [128, (H0 // 128) * 1024], BF16, allow_name_mangling=True) as wdb,
                nc.psum_tensor(None, [128, 2048], F32, allow_name_mangling=True) as pd,
            ):
                def f_wd(g):
                    src = wd_t.rearrange("(hc q) d -> q hc d", q=128)
                    p.dg.inc(g.dma_start(out=wdb.rearrange("q (hc d) -> q hc d", hc=H0 // 128),
                                         in_=src), 16)
                p.block.gpsimd(f_wd)
                wdn = p.dg.tok()
                dn_marks = {}
                for tc in range(TG // 128):
                    cabs = tg * (TG // 128) + tc
                    par = tc % 2
                    pdv = pd[:, par * 1024:(par + 1) * 1024]
                    emark = dn_marks.get(tc - 2)

                    def f_dn(t, pdv=pdv, tc=tc, wdn=wdn, emark=emark):
                        p.dg.wait(t, wdn)
                        if emark is not None:
                            t.wait_ge(p.dve.h, emark)
                        for hcabs in range(H0 // 128):
                            for nh in range(2):
                                p.pe.inc(t.matmul(pdv[:, nh * 512:(nh + 1) * 512],
                                         hsb[:, hcabs * TG + tc * 128: hcabs * TG + (tc + 1) * 128],
                                         wdb[:, hcabs * 1024 + nh * 512: hcabs * 1024 + (nh + 1) * 512],
                                         start=(hcabs == 0), stop=(hcabs == H0 // 128 - 1)))
                    p.block.tensor(f_dn)
                    dn = p.pe.n
                    xv = Xsb[:, cabs * 1024:(cabs + 1) * 1024]

                    def f_add(v, pdv=pdv, dn=dn, xv=xv):
                        v.wait_ge(p.pe.h, dn)
                        p.dve.inc(v.scalar_tensor_tensor(xv, pdv, 0.5 * QDEQ, xv, ALU.mult, ALU.add))
                    p.block.vector(f_add)
                    dn_marks[tc] = p.dve.n
            p.fence()


def _l0_scores(p, C, vtm, attnb, vpb, zcol, pa, qt_t, kt_t, hi, h, cbev, negC):
    exp_marks, ctx_marks = {}, {}
    for kc in range(NCH):
        av = attnb[:, (kc % 2) * 2048:(kc % 2 + 1) * 2048]
        emark = exp_marks.get(kc - 1)

        def f_smm(t, kc=kc, emark=emark):
            if kc == 0:
                t.wait_ge(p.dve.h, cbev)
            if emark is not None:
                t.wait_ge(p.act.h, emark)
            for qt in range(4):
                p.pe.inc(t.matmul(pa[:, qt * 512:(qt + 1) * 512],
                         kt_t[:, hi * T + kc * 128: hi * T + (kc + 1) * 128],
                         qt_t[:, hi * T + qt * 512: hi * T + (qt + 1) * 512],
                         start=True, stop=True))
        p.block.tensor(f_smm)
        smn = p.pe.n
        cmark = ctx_marks.get(kc - 2)

        def f_exp(s, kc=kc, smn=smn, av=av, cmark=cmark):
            s.wait_ge(p.pe.h, smn)
            if cmark is not None:
                s.wait_ge(p.pe.h, cmark)
            p.act.inc(s.activation(av, pa[:, 0:2048], AF.Exp, bias=negC,
                                   accum_out=zcol[:, kc % 2:kc % 2 + 1]))
        p.block.scalar(f_exp)
        exp_marks[kc] = p.act.n
        en = p.act.n
        vmark = ctx_marks.get(kc - 2)

        def f_vp(v, kc=kc, en=en, vmark=vmark):
            v.wait_ge(p.act.h, en)
            if vmark is not None:
                v.wait_ge(p.pe.h, vmark)
            p.dve.inc(v.reciprocal(zcol[:, 2:3], zcol[:, kc % 2:kc % 2 + 1]))
            v.wait_ge(p.dve.h, p.dve.n)
            vsl = vtm[:, kc * HEADS * VW + h * VW: kc * HEADS * VW + h * VW + 64]
            p.dve.inc(v.tensor_scalar(vpb[:, (kc % 2) * 64:(kc % 2 + 1) * 64],
                            vsl, zcol[:, 2:3], None, ALU.mult))
        p.block.vector(f_vp)
        vpn = p.dve.n

        def f_ctx(t, kc=kc, vpn=vpn, av=av):
            t.wait_ge(p.dve.h, vpn)
            for qt in range(4):
                p.pe.inc(t.matmul(pa[hi * 64:(hi + 1) * 64, 2048 + qt * 512:2048 + (qt + 1) * 512],
                         vpb[:, (kc % 2) * 64:(kc % 2 + 1) * 64],
                         av[:, qt * 512:(qt + 1) * 512],
                         start=(kc == 0), stop=(kc == NCH - 1)))
        p.block.tensor(f_ctx)
        ctx_marks[kc] = p.pe.n


def _l1_headhalf(p, C, vtm, attnb, rzb, zbc, ctmp, ctxfm, pa,
                 qt_t, kt_t, pr, hi, h, qh, cbev, qkready, negC, cdm):
    exp_marks, ctx_marks = {}, {}
    zprev = p.dve.n
    for kc in range(NCH):
        sub = kc % 2
        psc = pa[:, sub * 1024:(sub + 1) * 1024]
        av = attnb[:, sub * 1024:(sub + 1) * 1024]
        emark = exp_marks.get(kc - 2)

        def f_smm(t, kc=kc, psc=psc, emark=emark):
            if kc == 0:
                t.wait_ge(p.dve.h, cbev)
                t.wait_ge(p.dve.h, zprev)
                if qkready[0] == "dve":
                    t.wait_ge(p.dve.h, qkready[1])
            if emark is not None:
                t.wait_ge(p.act.h, emark)
            for qt in range(2):
                p.pe.inc(t.matmul(psc[:, qt * 512:(qt + 1) * 512],
                         kt_t[:, hi * T + kc * 128: hi * T + (kc + 1) * 128],
                         qt_t[:, hi * T + qh * 1024 + qt * 512: hi * T + qh * 1024 + (qt + 1) * 512],
                         start=True, stop=True))
        p.block.tensor(f_smm)
        smn = p.pe.n
        cmark = ctx_marks.get(kc - 2)

        def f_exp(s, smn=smn, psc=psc, av=av, cmark=cmark):
            s.wait_ge(p.pe.h, smn)
            if cmark is not None:
                s.wait_ge(p.pe.h, cmark)
            p.act.inc(s.activation(av, psc, AF.Exp, bias=negC))
        p.block.scalar(f_exp)
        exp_marks[kc] = p.act.n
        en = p.act.n

        def f_ctx(t, kc=kc, en=en, av=av):
            t.wait_ge(p.act.h, en)
            if kc == 0:
                t.wait_ge(p.dve.h, zprev)
            vsl = vtm[:, kc * HEADS * VW + h * VW: kc * HEADS * VW + (h + 1) * VW]
            for qt in range(2):
                p.pe.inc(t.matmul(pa[0:VW, 2048 + qt * 512:2048 + (qt + 1) * 512],
                         vsl, av[:, qt * 512:(qt + 1) * 512],
                         start=(kc == 0), stop=(kc == NCH - 1)))
        p.block.tensor(f_ctx)
        ctx_marks[kc] = p.pe.n
    ctxn = p.pe.n

    def f_z2(s, ctxn=ctxn):
        s.wait_ge(p.pe.h, ctxn)
        s.activation(pa[64:65, 3072:4096], pa[64:65, 2048:3072], AF.Ln)
        p.act.inc(s.activation(rzb[64:65, 0:1024], pa[64:65, 3072:4096], AF.Exp, scale=-1.0))
    p.block.scalar(f_z2)
    z2 = p.act.n

    def f_z3(t, z2=z2):
        t.wait_ge(p.act.h, z2)
        for qt in range(2):
            p.pe.inc(t.matmul(pa[0:64, 3072 + qt * 512:3072 + (qt + 1) * 512],
                     C["ones_bf"][64:65, 0:64],
                     rzb[64:65, qt * 512:(qt + 1) * 512], start=True, stop=True))
    p.block.tensor(f_z3)
    z3 = p.pe.n

    def f_z4(v, z3=z3, cdm=cdm):
        v.wait_ge(p.pe.h, z3)
        v.tensor_copy(zbc[:, :], pa[0:64, 3072:4096])
        dst = (ctxfm[0:64, pr * T + qh * 1024: pr * T + (qh + 1) * 1024]
               if hi == 0 else ctmp[:, :])
        if hi == 1 and cdm is not None:
            p.dw.wait(v, cdm)
        p.dve.inc(v.tensor_tensor(dst, pa[0:64, 2048:3072], zbc[:, :], ALU.mult))
    p.block.vector(f_z4)
    z4 = p.dve.n

    if hi == 1:
        def f_cdma(sy, z4=z4):
            sy.wait_ge(p.dve.h, z4)
            p.dw.inc(sy.dma_start(
                out=ctxfm[64:128, pr * T + qh * 1024: pr * T + (qh + 1) * 1024],
                in_=ctmp[:, :]), 16)
        p.block.sync(f_cdma)
        return p.dw.tok()
    return None



def emit_attn(p, C, Xsb, layer, wqkv, qkb=None, vb=None, wout=None, bout=None):
    nc = p.nc
    with (
        nc.sbuf_tensor(None, [128, DC * T], BF16, allow_name_mangling=True) as lnxfm,
        nc.sbuf_tensor(None, [128, NCH * HEADS * VW], BF16, allow_name_mangling=True) as vtm,
        nc.sbuf_tensor(None, [128, 8 * T], BF16, allow_name_mangling=True) as ctxfm,
    ):
        emit_ln(p, C, Xsb, lnxfm, 0, NCH)

        with (
            nc.sbuf_tensor(None, [128, DC * 1024], BF16, allow_name_mangling=True) as wv,
            nc.sbuf_tensor(None, [1, 1024], BF16, allow_name_mangling=True) as vbs,
            nc.psum_tensor(None, [128, 2048], F32, allow_name_mangling=True) as pv,
        ):
            def f_wv(g):
                src = wqkv.rearrange("(dc q) m -> q dc m", q=128)[:, :, 2048:3072]
                p.dg.inc(g.dma_start(out=wv.rearrange("q (dc m) -> q dc m", dc=DC), in_=src), 16)
                if vb is not None:
                    p.dg.inc(g.dma_start(out=vbs[:, :], in_=vb[:, :]), 16)
            p.block.gpsimd(f_wv)
            wvn = p.dg.tok()

            def f_ones(v):
                ov = vtm[:, :].rearrange("q (c h w) -> q c h w", h=HEADS, w=VW)[:, :, :, 64:65]
                p.dve.inc(v.memset(ov, 1.0))
            p.block.vector(f_ones)

            v_marks = {}
            for sc in range(NCH):
                par = sc % 2
                pvv = pv[:, par * 1024:(par + 1) * 1024]
                emark = v_marks.get(sc - 2)

                def f_vmm(t, sc=sc, pvv=pvv, wvn=wvn, emark=emark):
                    p.dg.wait(t, wvn)
                    if emark is not None:
                        t.wait_ge(p.act.h, emark)
                    for nh in range(2):
                        last = None
                        for dc in range(DC):
                            last = t.matmul(pvv[:, nh * 512:(nh + 1) * 512],
                                            lnxfm[:, dc * T + sc * 128: dc * T + (sc + 1) * 128],
                                            wv[:, dc * 1024 + nh * 512: dc * 1024 + (nh + 1) * 512],
                                            start=(dc == 0), stop=(dc == DC - 1 and vb is None))
                        if vb is not None:
                            last = t.matmul(pvv[:, nh * 512:(nh + 1) * 512], C["ones_bf"][0:1, 0:128],
                                            vbs[0:1, nh * 512:(nh + 1) * 512], start=False, stop=True)
                        p.pe.inc(last)
                p.block.tensor(f_vmm)
                vn = p.pe.n
                dstv = vtm[:, :].rearrange("q (c h w) -> q c h w", h=HEADS, w=VW)[:, sc, :, 0:64]

                def f_vev(s, pvv=pvv, vn=vn, dstv=dstv):
                    s.wait_ge(p.pe.h, vn)
                    p.act.inc(s.activation(dstv, pvv.rearrange("q (h w) -> q h w", w=64), AF.Copy))
                p.block.scalar(f_vev)
                v_marks[sc] = p.act.n
        p.fence()

        with ExitStack() as es:
            _n = [0]

            def _t(shape, dt):
                _n[0] += 1
                return es.enter_context(
                    nc.sbuf_tensor("apx%d" % _n[0], shape, dt, allow_name_mangling=True))
            wqk = _t([128, 2 * 2 * DC * 128], BF16)
            qkbs = _t([128, 16], F32)
            qtb = _t([64, 2 * T], BF16)
            ktb = _t([64, 2 * T], BF16)
            ktmp = _t([128, 2 * 512], BF16)
            attnb = _t([128, (4096 if layer == 0 else 2048)], BF16)
            sqb = attnb[0:64, 0:2048]
            vpb = _t([128, 2 * 64], BF16)
            zcol = _t([128, 8], F32)
            rzb = _t([128, (1024 if layer == 1 else 1)], BF16)
            zbc = _t([64, (1024 if layer == 1 else 1)], BF16)
            ctmp = _t([64, (1024 if layer == 1 else 1)], BF16)
            sc1 = _t([1, 8], F32)
            cosb = _t([32, (2048 if layer == 1 else 1)], BF16)
            sinb = _t([32, (2048 if layer == 1 else 1)], BF16)
            pa = es.enter_context(
                nc.psum_tensor("apx_pa", [128, 4096], F32, allow_name_mangling=True))
            if layer == 1:
                def f_rt(sy):
                    p.dw.inc(sy.dma_start(out=cosb[:, :], in_=C["ropecos"][:, :]), 16)
                    p.dw.inc(sy.dma_start(out=sinb[:, :], in_=C["ropesin"][:, :]), 16)
                p.block.sync(f_rt)
            if qkb is not None:
                def f_qkb(g):
                    p.dg.inc(g.dma_start(out=qkbs[:, :], in_=qkb[:, :]), 16)
                p.block.gpsimd(f_qkb)
                bn = p.dg.tok()

                def f_qkbs(v):
                    p.dg.wait(v, bn)
                    p.dve.inc(v.tensor_tensor(qkbs[:, :], qkbs[:, :],
                                    C["qkscale"][:, :].broadcast_to([128, 16]), ALU.mult))
                    v.wait_ge(p.dve.h, p.dve.n)
                p.block.vector(f_qkbs)

            pair_done = {}
            for pr in range(8):
                par = pr % 2
                wv_ = wqk[:, par * 2 * DC * 128:(par + 1) * 2 * DC * 128]
                qt_t = qtb[:, 0:2 * T]
                kt_t = ktb[:, 0:2 * T]
                prev = pair_done.get(pr - 2)
                prev1 = pair_done.get(pr - 1)

                def f_wqk(g, pr=pr, wv_=wv_, prev=prev):
                    if prev is not None:
                        g.wait_ge(p.pe.h, prev[0])
                    for hi, h in enumerate((2 * pr, 2 * pr + 1)):
                        for side, off in ((0, 0), (1, 1024)):
                            src = wqkv.rearrange("(dc q) m -> q dc m", q=128)[:, :, off + h * 64: off + (h + 1) * 64]
                            dst = wv_.rearrange("q (dc w) -> q dc w", dc=DC)[:, :, hi * 128 + side * 64: hi * 128 + (side + 1) * 64]
                            p.dg.inc(g.dma_start(out=dst, in_=src), 16)
                p.block.gpsimd(f_wqk)
                wn = p.dg.tok()

                ev_marks = {}
                kd_marks = {}
                for hi in range(2):
                    h = 2 * pr + hi
                    for qt in range(4):
                        i = hi * 4 + qt
                        sub = i % 2
                        pq = pa[:, sub * 512:(sub + 1) * 512]
                        emark = ev_marks.get(i - 2)

                        def f_qkm(t, wv_=wv_, hi=hi, qt=qt, wn=wn, pq=pq, emark=emark, prev1=prev1):
                            p.dg.wait(t, wn)
                            if emark is not None:
                                t.wait_ge(p.dve.h, emark)
                            elif prev1 is not None:
                                t.wait_ge(p.act.h, prev1[1])
                                t.wait_ge(p.dve.h, prev1[2])
                                p.dw.wait(t, prev1[3])
                            wr = wv_.rearrange("q (dc w) -> q dc w", dc=DC)
                            for dc in range(DC):
                                p.pe.inc(t.matmul(pq, wr[:, dc, hi * 128:(hi + 1) * 128],
                                         lnxfm[:, dc * T + qt * 512: dc * T + (qt + 1) * 512],
                                         start=(dc == 0), stop=(dc == DC - 1)))
                        p.block.tensor(f_qkm)
                        qn = p.pe.n
                        kmark = kd_marks.get(i - 2)

                        def f_qkev(v, pq=pq, qn=qn, hi=hi, qt=qt, h=h, qt_t=qt_t,
                                   i=i, kmark=kmark):
                            v.wait_ge(p.pe.h, qn)
                            if kmark is not None:
                                p.dw.wait(v, kmark)
                            ks = ktmp[64:128, (i % 2) * 512:(i % 2 + 1) * 512]
                            if qkb is not None:
                                v.tensor_scalar(qt_t[:, hi * T + qt * 512: hi * T + (qt + 1) * 512],
                                                pq[0:64, :], qkbs[0:64, h:h + 1], 0.125,
                                                ALU.add, ALU.mult)
                                p.dve.inc(v.tensor_scalar(ks, pq[64:128, :], qkbs[64:128, h:h + 1],
                                                          None, ALU.add))
                            else:
                                v.tensor_scalar_mul(qt_t[:, hi * T + qt * 512: hi * T + (qt + 1) * 512],
                                                    pq[0:64, :], 0.125)
                                p.dve.inc(v.tensor_copy(ks, pq[64:128, :]))
                        p.block.vector(f_qkev)
                        ev_marks[i] = p.dve.n
                        evn = p.dve.n

                        def f_kd(sy, hi=hi, qt=qt, i=i, evn=evn, kt_t=kt_t, prev1=prev1):
                            sy.wait_ge(p.dve.h, evn)
                            if i < 2 and prev1 is not None:
                                sy.wait_ge(p.pe.h, prev1[0])
                            p.dw.inc(sy.dma_start(
                                out=kt_t[:, hi * T + qt * 512: hi * T + (qt + 1) * 512],
                                in_=ktmp[64:128, (i % 2) * 512:(i % 2 + 1) * 512]), 16)
                        p.block.sync(f_kd)
                        kd_marks[i] = p.dw.tok()
                qkprojn = p.dve.n
                kdman = p.dw.tok()
                qkready = ("dve", qkprojn)

                if layer == 1:
                    for ti, tile in enumerate((qt_t, kt_t)):
                        for hi in range(2):
                            for ch in range(2):
                                co = hi * T + ch * 1024
                                a1 = tile[0:32, co:co + 1024]
                                a2 = tile[32:64, co:co + 1024]
                                A = attnb[0:32, 0:1024]
                                B = attnb[0:32, 1024:2048]
                                S2 = rzb[0:32, 0:1024]
                                cosv = cosb[:, ch * 1024:(ch + 1) * 1024]
                                sinv = sinb[:, ch * 1024:(ch + 1) * 1024]
                                dve_pre = p.dve.n

                                def f_rd(sy, a2=a2, S2=S2, kdman=kdman, dve_pre=dve_pre,
                                         first=(ti == 0 and hi == 0 and ch == 0)):
                                    if first:
                                        p.dw.wait(sy, kdman)
                                    sy.wait_ge(p.dve.h, dve_pre)
                                    p.dw.inc(sy.dma_start(out=S2, in_=a2), 16)
                                p.block.sync(f_rd)
                                rdn = p.dw.tok()

                                def f_r1(v, a1=a1, A=A, B=B, S2=S2, cosv=cosv,
                                         sinv=sinv, rdn=rdn):
                                    p.dw.wait(v, rdn)
                                    v.tensor_mul(A, a1, sinv)
                                    v.tensor_mul(B, S2, cosv)
                                    v.tensor_add(B, A, B)
                                    v.tensor_mul(A, a1, cosv)
                                    v.tensor_mul(S2, S2, sinv)
                                    p.dve.inc(v.tensor_sub(a1, A, S2))
                                p.block.vector(f_r1)
                                r1n = p.dve.n

                                def f_wr(sy, a2=a2, B=B, r1n=r1n):
                                    sy.wait_ge(p.dve.h, r1n)
                                    p.dw.inc(sy.dma_start(out=a2, in_=B), 16)
                                p.block.sync(f_wr)
                    rope_tok = p.dw.tok()

                    def f_rw(v, rope_tok=rope_tok):
                        p.dw.wait(v, rope_tok)
                        p.dve.inc(v.memset(zcol[:, 5:6], 0.0))
                    p.block.vector(f_rw)
                    qkready = ("dve", p.dve.n)

                act_snap, dve_snap = p.act.n, p.dve.n

                def f_c0(v, qkready=qkready, act_snap=act_snap, kdman=kdman):
                    v.wait_ge(p.act.h, act_snap)
                    p.dw.wait(v, kdman)
                    p.dve.inc(v.memset(sc1[:, 0:2], 0.0))
                p.block.vector(f_c0)

                for side, tile in ((0, qtb), (1, ktb)):
                    tv = tile[:, 0:2 * T]
                    for half in range(2):
                        pe_now = p.pe.n

                        def f_sq(v, tv=tv, half=half, pe_now=pe_now):
                            v.wait_ge(p.pe.h, pe_now)
                            p.dve.inc(v.tensor_mul(sqb[:, :], tv[:, half * T:(half + 1) * T],
                                         tv[:, half * T:(half + 1) * T]))
                        p.block.vector(f_sq)
                        sqn = p.dve.n
                        dve_now = p.dve.n

                        def f_nmm(t, sqn=sqn, act_snap=act_snap, dve_now=dve_now):
                            t.wait_ge(p.dve.h, sqn)
                            t.wait_ge(p.act.h, act_snap)
                            t.wait_ge(p.dve.h, dve_now)
                            for qt in range(4):
                                p.pe.inc(t.matmul(pa[0:1, 2048 + qt * 512:2048 + (qt + 1) * 512],
                                         C["ones_bf"][0:64, 0:1],
                                         sqb[:, qt * 512:(qt + 1) * 512], start=True, stop=True))
                        p.block.tensor(f_nmm)
                        nmn = p.pe.n

                        def f_nred(v, side=side, nmn=nmn):
                            v.wait_ge(p.pe.h, nmn)
                            p.dve.inc(v.tensor_reduce(sc1[0:1, 4:5], pa[0:1, 2048:4096], AX.X, ALU.max))
                            v.wait_ge(p.dve.h, p.dve.n)
                            p.dve.inc(v.tensor_tensor(sc1[0:1, side:side + 1], sc1[0:1, side:side + 1],
                                            sc1[0:1, 4:5], ALU.max))
                        p.block.vector(f_nred)
                nrn = p.dve.n

                def f_cln(s, nrn=nrn):
                    s.wait_ge(p.dve.h, nrn)
                    p.act.inc(s.activation(sc1[0:1, 0:2], sc1[0:1, 0:2], AF.Ln))
                p.block.scalar(f_cln)
                cn = p.act.n

                def f_cc(v, cn=cn):
                    v.wait_ge(p.act.h, cn)
                    p.dve.inc(v.tensor_add(sc1[0:1, 5:6], sc1[0:1, 0:1], sc1[0:1, 1:2]))
                    v.wait_ge(p.dve.h, p.dve.n)
                    p.dve.inc(v.tensor_scalar(sc1[0:1, 5:6], sc1[0:1, 5:6], 0.5, 0.1, ALU.mult, ALU.add))
                p.block.vector(f_cc)
                ccn = p.dve.n

                def f_cexp(s, ccn=ccn):
                    s.wait_ge(p.dve.h, ccn)
                    p.act.inc(s.activation(sc1[0:1, 6:7], sc1[0:1, 5:6], AF.Exp))
                p.block.scalar(f_cexp)
                cen = p.act.n

                def f_cneg(v, cen=cen):
                    v.wait_ge(p.act.h, cen)
                    p.dve.inc(v.tensor_scalar_mul(sc1[0:1, 7:8], sc1[0:1, 6:7], -1.0))
                p.block.vector(f_cneg)
                cnn = p.dve.n

                def f_cbc(t, cnn=cnn):
                    t.wait_ge(p.dve.h, cnn)
                    p.pe.inc(t.matmul(pa[:, 2048:2049], C["ones_f32"][0:1, :], sc1[0:1, 7:8],
                             start=True, stop=True))
                p.block.tensor(f_cbc)
                cbn = p.pe.n

                def f_cbe(v, cbn=cbn):
                    v.wait_ge(p.pe.h, cbn)
                    p.dve.inc(v.tensor_copy(zcol[:, 4:5], pa[:, 2048:2049]))
                p.block.vector(f_cbe)
                cbev = p.dve.n
                negC = zcol[:, 4:5]

                if layer == 0:
                    for hi in range(2):
                        _l0_scores(p, C, vtm, attnb, vpb, zcol, pa, qt_t, kt_t,
                                   hi, 2 * pr + hi, cbev, negC)
                    ctxn = p.pe.n

                    def f_ctxev(s, pr=pr, ctxn=ctxn):
                        s.wait_ge(p.pe.h, ctxn)
                        p.act.inc(s.activation(ctxfm[:, pr * T:(pr + 1) * T], pa[:, 2048:4096], AF.Copy))
                    p.block.scalar(f_ctxev)
                    pair_done[pr] = (p.pe.n, p.act.n, p.dve.n, p.dw.tok())
                else:
                    cdm = None
                    for hi in range(2):
                        for qh in range(2):
                            cdm2 = _l1_headhalf(p, C, vtm, attnb, rzb, zbc, ctmp,
                                                ctxfm, pa, qt_t, kt_t, pr, hi,
                                                2 * pr + hi, qh, cbev, qkready, negC, cdm)
                            if cdm2 is not None:
                                cdm = cdm2
                    pair_done[pr] = (p.pe.n, p.act.n, p.dve.n, p.dw.tok())
        p.fence()

        with (
            nc.sbuf_tensor(None, [128, 8 * 1024], BF16, allow_name_mangling=True) as wob,
            nc.sbuf_tensor(None, [1, 1024], BF16, allow_name_mangling=True) as bob,
            nc.psum_tensor(None, [128, 2048], F32, allow_name_mangling=True) as po,
        ):
            def f_wo(g):
                src = wout.rearrange("(ch q) d -> q ch d", q=128)
                p.dg.inc(g.dma_start(out=wob.rearrange("q (ch d) -> q ch d", ch=8), in_=src), 16)
                p.dg.inc(g.dma_start(out=bob[:, :], in_=bout[:, :]), 16)
            p.block.gpsimd(f_wo)
            won = p.dg.tok()
            o_marks = {}
            for sc in range(NCH):
                par = sc % 2
                pov = po[:, par * 1024:(par + 1) * 1024]
                emark = o_marks.get(sc - 2)

                def f_omm(t, sc=sc, pov=pov, won=won, emark=emark):
                    p.dg.wait(t, won)
                    if emark is not None:
                        t.wait_ge(p.dve.h, emark)
                    for nh in range(2):
                        for ch in range(8):
                            t.matmul(pov[:, nh * 512:(nh + 1) * 512],
                                     ctxfm[:, ch * T + sc * 128: ch * T + (sc + 1) * 128],
                                     wob[:, ch * 1024 + nh * 512: ch * 1024 + (nh + 1) * 512],
                                     start=(ch == 0), stop=False)
                        p.pe.inc(t.matmul(pov[:, nh * 512:(nh + 1) * 512], C["ones_bf"][0:1, 0:128],
                                 bob[0:1, nh * 512:(nh + 1) * 512], start=False, stop=True))
                p.block.tensor(f_omm)
                on = p.pe.n
                xv = Xsb[:, sc * 1024:(sc + 1) * 1024]

                def f_oadd(v, pov=pov, on=on, xv=xv):
                    v.wait_ge(p.pe.h, on)
                    p.dve.inc(v.tensor_add(xv, xv, pov))
                p.block.vector(f_oadd)
                o_marks[sc] = p.dve.n
        p.fence()


def emit_gating(p, C, lnxfm, wgate, bgv, logit_sb):
    nc = p.nc
    with (
        nc.sbuf_tensor(None, [128, DC * 8], BF16, allow_name_mangling=True) as wgb,
        nc.sbuf_tensor(None, [1, 8], F32, allow_name_mangling=True) as bgs,
        nc.sbuf_tensor(None, [128, 8], F32, allow_name_mangling=True) as bgb,
        nc.psum_tensor(None, [128, 2048], F32, allow_name_mangling=True) as pg,
    ):
        def f_w(g):
            src = wgate.rearrange("(dc q) e -> q dc e", q=128)
            p.dg.inc(g.dma_start(out=wgb.rearrange("q (dc e) -> q dc e", dc=DC), in_=src), 16)
            p.dg.inc(g.dma_start(out=bgs[:, :], in_=bgv[:, :]), 16)
        p.block.gpsimd(f_w)
        wn = p.dg.tok()

        def f_bg(t):
            p.dg.wait(t, wn)
            p.pe.inc(t.matmul(pg[:, 0:8], C["ones_f32"][0:1, :], bgs[0:1, :], start=True, stop=True))
        p.block.tensor(f_bg)
        bn = p.pe.n

        def f_bge(v):
            v.wait_ge(p.pe.h, bn)
            p.dve.inc(v.tensor_copy(bgb[:, :], pg[:, 0:8]))
        p.block.vector(f_bge)
        bev = p.dve.n

        ev_marks = {}
        for tc in range(NCH):
            emark = ev_marks.get(tc - 4)
            slot = (tc % 4) * 512

            def f_mm(t, tc=tc, wn=wn, emark=emark, slot=slot):
                p.dg.wait(t, wn)
                if emark is not None:
                    t.wait_ge(p.dve.h, emark)
                else:
                    t.wait_ge(p.dve.h, bev)
                for dc in range(DC):
                    p.pe.inc(t.matmul(pg[:, slot:slot + 8],
                             lnxfm[:, dc * T + tc * 128: dc * T + (tc + 1) * 128],
                             wgb[:, dc * 8:(dc + 1) * 8],
                             start=(dc == 0), stop=(dc == DC - 1)))
            p.block.tensor(f_mm)
            mn = p.pe.n

            def f_ev(v, tc=tc, mn=mn, slot=slot):
                v.wait_ge(p.pe.h, mn)
                v.wait_ge(p.dve.h, bev)
                p.dve.inc(v.tensor_add(logit_sb[:, tc * 8:(tc + 1) * 8], pg[:, slot:slot + 8],
                             bgb[:, :]))
            p.block.vector(f_ev)
            ev_marks[tc] = p.dve.n
    p.fence()



def emit_routing(p, C, lg_d, stk):
    nc = p.nc
    es = stk.enter_context
    lg = es(nc.sbuf_tensor(None, [128, TCH * 8], F32, allow_name_mangling=True))
    lab = es(nc.sbuf_tensor(None, [128, TCH * 8], BF16, allow_name_mangling=True))
    scr = es(nc.sbuf_tensor(None, [128, TCH * 8], F32, allow_name_mangling=True))
    seqs = es(nc.sbuf_tensor(None, [128, 8 * 128], F32, allow_name_mangling=True))
    labT = es(nc.sbuf_tensor(None, [128, 8 * 128], F32, allow_name_mangling=True))
    mx = es(nc.sbuf_tensor(None, [128, 2 * TCH], F32, allow_name_mangling=True))
    offs = es(nc.sbuf_tensor(None, [128, 8], F32, allow_name_mangling=True))
    bt = es(nc.sbuf_tensor(None, [128, 8], F32, allow_name_mangling=True))
    row = es(nc.sbuf_tensor(None, [8, 2 * 128], F32, allow_name_mangling=True))

    def f_ld(g):
        p.dg.inc(g.dma_start(out=lg[:, :], in_=lg_d[:, :]), 16)
    p.block.gpsimd(f_ld)
    ln_ = p.dg.tok()

    lgv = lg[:, :].rearrange("q (c e) -> q c e", e=8)
    scrv = scr[:, :].rearrange("q (c e) -> q c e", e=8)
    labv = lab[:, :].rearrange("q (c e) -> q c e", e=8)
    m1 = mx[:, 0:TCH]
    m2 = mx[:, TCH:2 * TCH]

    def f_top2(v):
        p.dg.wait(v, ln_)
        p.dve.inc(v.tensor_reduce(m1, lgv, AX.X, ALU.max))
        v.wait_ge(p.dve.h, p.dve.n)
        m1b = m1.rearrange("q c -> q c ()").broadcast_to([128, TCH, 8])
        p.dve.inc(v.tensor_tensor(scrv, lgv, m1b, ALU.is_equal))
        v.wait_ge(p.dve.h, p.dve.n)
        p.dve.inc(v.scalar_tensor_tensor(scr[:, :], scr[:, :], -1e30, lg[:, :], ALU.mult, ALU.add))
        v.wait_ge(p.dve.h, p.dve.n)
        p.dve.inc(v.tensor_reduce(m2, scrv, AX.X, ALU.max))
        v.wait_ge(p.dve.h, p.dve.n)
        m2b = m2.rearrange("q c -> q c ()").broadcast_to([128, TCH, 8])
        p.dve.inc(v.tensor_tensor(scrv, scrv, m2b, ALU.is_equal))
        v.wait_ge(p.dve.h, p.dve.n)
        p.dve.inc(v.tensor_tensor(labv, lgv, m1b, ALU.is_equal))
        v.wait_ge(p.dve.h, p.dve.n)
        p.dve.inc(v.tensor_tensor(lab[:, :], lab[:, :], scr[:, :], ALU.add))
    p.block.vector(f_top2)
    topn = p.dve.n

    with (
        nc.psum_tensor(None, [128, 1024], BF16, allow_name_mangling=True) as ptb,
        nc.psum_tensor(None, [128, 512], F32, allow_name_mangling=True) as ptf,
    ):
        sc_marks = {}
        for e in range(8):
            def f_tr(t, e=e, topn=topn):
                if e == 0:
                    t.wait_ge(p.dve.h, topn)
                else:
                    t.wait_ge(p.dve.h, sc_marks[e - 1])
                src = lab[:, :].rearrange("q (c e) -> q c e", e=8)[:, :, e]
                p.pe.inc(t.transpose(ptb[:, 0:128], src, C["id128"][:, :]))
            p.block.tensor(f_tr)
            tn = p.pe.n

            def f_sc(v, e=e, tn=tn):
                v.wait_ge(p.pe.h, tn)
                p.dve.inc(v.tensor_copy(labT[:, e * 128:(e + 1) * 128], ptb[:, 0:128]))
                v.wait_ge(p.dve.h, p.dve.n)
                p.dve.inc(v.tensor_tensor_scan(seqs[:, e * 128:(e + 1) * 128],
                                     labT[:, e * 128:(e + 1) * 128],
                                     labT[:, e * 128:(e + 1) * 128], 0.0, ALU.add, ALU.bypass))
                v.wait_ge(p.dve.h, p.dve.n)
                p.dve.inc(v.tensor_copy(bt[:, e:e + 1], seqs[:, e * 128 + 127:e * 128 + 128]))
            p.block.vector(f_sc)
            sc_marks[e] = p.dve.n
        scn = p.dve.n

        def f_t2(t, scn=scn):
            t.wait_ge(p.dve.h, scn)
            p.pe.inc(t.transpose(ptf[:, 0:128].rearrange("a b -> a b") if False else ptf[0:8, 0:128],
                        bt[:, 0:8], C["id128f"][:, :]))
        p.block.tensor(f_t2)
        t2n = p.pe.n

        def f_off(v, t2n=t2n):
            v.wait_ge(p.pe.h, t2n)
            p.dve.inc(v.memset(row[0:8, 0:1], 0.0))
            p.dve.inc(v.tensor_copy(row[0:8, 1:128], ptf[0:8, 0:127]))
            v.wait_ge(p.dve.h, p.dve.n)
            p.dve.inc(v.tensor_tensor_scan(row[0:8, 128:256], row[0:8, 0:128], row[0:8, 0:128],
                                 0.0, ALU.add, ALU.bypass))
        p.block.vector(f_off)
        offn = p.dve.n

        def f_t3(t, offn=offn):
            t.wait_ge(p.dve.h, offn)
            p.pe.inc(t.transpose(ptf[:, 128:136], row[0:8, 128:256], C["id128f"][0:8, 0:8]))
        p.block.tensor(f_t3)
        t3n = p.pe.n

        def f_oe(v, t3n=t3n):
            v.wait_ge(p.pe.h, t3n)
            p.dve.inc(v.tensor_copy(offs[:, :], ptf[:, 128:136]))
            v.wait_ge(p.dve.h, p.dve.n)
            for e in range(8):
                p.dve.inc(v.tensor_scalar(seqs[:, e * 128:(e + 1) * 128], seqs[:, e * 128:(e + 1) * 128],
                                offs[:, e:e + 1], None, ALU.add))
        p.block.vector(f_oe)
    p.fence()
    return {"lg": lg, "lab": lab, "scr": scr, "seqs": seqs, "labT": labT, "mx": mx}



def build_l1(stages="fafg"):
    nc = bass.Bass("TRN2", target_bir_lowering=False)
    C = make_consts(nc)
    x_d = nc.dram_tensor("x", [T, D], F32, kind="ExternalInput").ap()
    wg_d = nc.dram_tensor("wg_t", [D, H0], I32, kind="ExternalInput").ap()
    wu_d = nc.dram_tensor("wu_t", [D, H0], I32, kind="ExternalInput").ap()
    wd_d = nc.dram_tensor("wd_t", [H0, D], I32, kind="ExternalInput").ap()
    wqkv0_d = nc.dram_tensor("wqkv0", [D, 3 * D], F32, kind="ExternalInput").ap()
    qkb_d = nc.dram_tensor("qkb0", [128, 16], F32, kind="ExternalInput").ap()
    vb_d = nc.dram_tensor("vb0", [1, D], F32, kind="ExternalInput").ap()
    lw_d = nc.dram_tensor("lw_t", [D, D], F32, kind="ExternalInput").ap()
    lb_d = nc.dram_tensor("lb0", [1, D], F32, kind="ExternalInput").ap()
    wgate_d = nc.dram_tensor("wgate", [D, 8], F32, kind="ExternalInput").ap()
    bg_d = nc.dram_tensor("bg", [1, 8], F32, kind="ExternalInput").ap()
    xo_d = nc.dram_tensor("xo", [T, D], F32, kind="ExternalOutput").ap()
    lnx_d = nc.dram_tensor("lnx", [T, D], BF16, kind="ExternalOutput").ap()
    lgo_d = nc.dram_tensor("lg", [T, 8], F32, kind="ExternalOutput").ap()

    with ExitStack() as stack:
        Xsb = stack.enter_context(nc.sbuf_tensor("Xsb", [128, NCH * 1024], F32))
        block = stack.enter_context(nc.Block())
        p = P(nc, block, stack)
        load_consts(p, C, ["id128", "ones_bf", "ones_f32", "qkscale"])

        def f_ldx(sy):
            p.dw.inc(sy.dma_start(out=Xsb[:, :].rearrange("q (c d) -> q c d", c=NCH),
                                  in_=x_d.rearrange("(c q) d -> q c d", q=128)), 16)
        p.block.sync(f_ldx)
        p.fence()

        if "f" in stages:
            emit_ffn0(p, C, Xsb[:, :], wg_d, wu_d, wd_d)
        if "a" in stages:
            emit_attn(p, C, Xsb[:, :], 0, wqkv0_d, qkb=qkb_d, vb=vb_d, wout=lw_d, bout=lb_d)
        if stages.count("f") > 1:
            emit_ffn0(p, C, Xsb[:, :], wg_d, wu_d, wd_d)

        with (
            nc.sbuf_tensor("lnxf4", [128, DC * T], BF16) as lnxfm,
            nc.sbuf_tensor("lnxt4", [128, NCH * 1024], BF16) as lnxtm,
            nc.sbuf_tensor("lgsb", [128, NCH * 8], F32) as lgsb,
        ):
            emit_ln(p, C, Xsb[:, :], lnxfm[:, :], 0, NCH, lnxtm=lnxtm[:, :])
            if "g" in stages:
                emit_gating(p, C, lnxfm[:, :], wgate_d, bg_d, lgsb[:, :])

            def f_st(sy):
                p.dw.inc(sy.dma_start(out=xo_d.rearrange("(c q) d -> q c d", q=128),
                                      in_=Xsb[:, :].rearrange("q (c d) -> q c d", c=NCH)), 16)
                p.dw.inc(sy.dma_start(out=lnx_d.rearrange("(c q) d -> q c d", q=128),
                                      in_=lnxtm[:, :].rearrange("q (c d) -> q c d", c=NCH)), 16)
                p.dw.inc(sy.dma_start(out=lgo_d.rearrange("(c q) e -> q c e", q=128),
                                      in_=lgsb[:, :].rearrange("q (c e) -> q c e", c=NCH)), 16)
                p.dw.wait(sy, p.dw.tok())
            p.block.sync(f_st)
    return nc


def build_l2a():
    nc = bass.Bass("TRN2", target_bir_lowering=False)
    C = make_consts(nc)
    lg_d = nc.dram_tensor("lga", [128, TCH * 8], F32, kind="ExternalInput").ap()
    lnxa_d = nc.dram_tensor("lnxa", [TALL + 1, D], BF16, kind="ExternalInput").ap()
    w1_d = nc.dram_tensor("w1", [D, HM], F32, kind="ExternalInput").ap()
    w2_d = nc.dram_tensor("w2", [D, HM], F32, kind="ExternalInput").ap()
    w3_d = nc.dram_tensor("w3", [HM, D], F32, kind="ExternalInput").ap()
    esel_d = nc.dram_tensor("esel", [128, 8], F32, kind="ExternalInput").ap()
    o_d = nc.dram_tensor("o", [CAP, D], F32, kind="ExternalOutput").ap()
    g_dram = nc.dram_tensor("gbuf", [CAP + 1, 2], I32, kind="Internal").ap()
    h_dram = nc.dram_tensor("hbuf", [HM // 128, 128, CAP], BF16, kind="Internal").ap()
    ginit = nc.inline_tensor(np.full((CAP + 1, 2), SENT, np.int32), "c_ginit").ap()

    with ExitStack() as stack:
        block = stack.enter_context(nc.Block())
        p = P(nc, block, stack)
        load_consts(p, C, ["id128", "id128f", "tid2"])
        rstk = stack.enter_context(ExitStack())
        r = emit_routing(p, C, lg_d, rstk)
        seqs, labT = r["seqs"], r["labT"]

        with (
            nc.sbuf_tensor("ids", [128, 4 * 128], F32) as ids,
            nc.sbuf_tensor("idsi", [128, 128], I32) as idsi,
            nc.sbuf_tensor("esel_sb", [128, 8], F32) as esel,
        ):
            def f_le(g):
                p.dg.inc(g.dma_start(out=esel[:, :], in_=esel_d[:, :]), 16)
                p.dg.inc(g.dma_start(out=g_dram[:, :], in_=ginit[:, :]), 16)
            p.block.gpsimd(f_le)
            len_ = p.dg.tok()

            def f_idx(v):
                p.dg.wait(v, len_)
                sq = ids[:, 0:128]
                lb = ids[:, 128:256]
                vm = ids[:, 256:384]
                ix = ids[:, 384:512]
                p.dve.inc(v.memset(sq, 0.0))
                p.dve.inc(v.memset(lb, 0.0))
                v.wait_ge(p.dve.h, p.dve.n)
                for e in range(8):
                    p.dve.inc(v.scalar_tensor_tensor(sq, seqs[:, e * 128:(e + 1) * 128],
                                           esel[:, e:e + 1], sq, ALU.mult, ALU.add))
                    p.dve.inc(v.scalar_tensor_tensor(lb, labT[:, e * 128:(e + 1) * 128],
                                           esel[:, e:e + 1], lb, ALU.mult, ALU.add))
                    v.wait_ge(p.dve.h, p.dve.n)
                p.dve.inc(v.tensor_single_scalar(vm, sq, float(CAP), ALU.is_le))
                v.wait_ge(p.dve.h, p.dve.n)
                p.dve.inc(v.tensor_mul(vm, vm, lb))
                v.wait_ge(p.dve.h, p.dve.n)
                p.dve.inc(v.tensor_scalar_add(ix, sq, -float(CAP + 1)))
                v.wait_ge(p.dve.h, p.dve.n)
                p.dve.inc(v.tensor_mul(ix, ix, vm))
                v.wait_ge(p.dve.h, p.dve.n)
                p.dve.inc(v.tensor_scalar_add(ix, ix, float(CAP)))
                v.wait_ge(p.dve.h, p.dve.n)
                p.dve.inc(v.tensor_copy(idsi[:, :], ix))
            p.block.vector(f_idx)
            idn = p.dve.n

            def f_scat(g, idn=idn):
                g.wait_ge(p.dve.h, idn)
                for f in range(128):
                    p.dg.inc(g.indirect_dma_start(
                        g_dram[0:CAP + 1, :], IndirectOffsetOnAxis(idsi[:, f:f + 1], 0),
                        C["tid2"][:, 2 * f:2 * f + 2], None), 16)
            p.block.gpsimd(f_scat)
        p.fence()
        rstk.close()

        with (
            nc.sbuf_tensor("gsb", [128, 2 * NSB], I32) as gsb,
            nc.sbuf_tensor("xett", [128, 2 * 1024], BF16) as xett,
            nc.sbuf_tensor("xe", [128, DC * CAP], BF16) as xe,
        ):
            ptx_cm = nc.psum_tensor("ptx", [128, 2048], BF16)
            ptx = ptx_cm.__enter__()

            def f_gl(g):
                src = g_dram[0:CAP, :].rearrange("(b q) w -> q b w", q=128)
                p.dg.inc(g.dma_start(out=gsb[:, :].rearrange("q (b w) -> q b w", w=2), in_=src), 16)
            p.block.gpsimd(f_gl)
            gln = p.dg.tok()

            tr_marks, te_marks = {}, {}
            for b in range(NSB):
                def f_ga(g, b=b, gln=gln):
                    p.dg.wait(g, gln)
                    if b >= 2:
                        g.wait_ge(p.pe.h, tr_marks[b - 2])
                    p.dg.inc(g.indirect_dma_start(
                        xett[:, (b % 2) * 1024:(b % 2 + 1) * 1024], None,
                        lnxa_d[0:TALL + 1, :],
                        IndirectOffsetOnAxis(gsb[:, 2 * b:2 * b + 1], 0)), 16)
                p.block.gpsimd(f_ga)
                gn = p.dg.tok()

                def f_tr(t, b=b, gn=gn):
                    p.dg.wait(t, gn)
                    if b >= 2:
                        t.wait_ge(p.dve.h, te_marks[b - 2])
                    for dc in range(DC):
                        p.pe.inc(t.transpose(ptx[:, (b % 2) * 1024 + dc * 128:(b % 2) * 1024 + (dc + 1) * 128],
                                    xett[:, (b % 2) * 1024 + dc * 128:(b % 2) * 1024 + (dc + 1) * 128],
                                    C["id128"][:, :]))
                p.block.tensor(f_tr)
                tr_marks[b] = p.pe.n
                tn = p.pe.n

                def f_te(v, b=b, tn=tn):
                    v.wait_ge(p.pe.h, tn)
                    for dc in range(DC):
                        p.dve.inc(v.tensor_copy(xe[:, dc * CAP + b * 128: dc * CAP + (b + 1) * 128],
                                      ptx[:, (b % 2) * 1024 + dc * 128:(b % 2) * 1024 + (dc + 1) * 128]))
                p.block.vector(f_te)
                te_marks[b] = p.dve.n
            p.fence()
            ptx_cm.__exit__(None, None, None)

            HGA = 512
            with (
                nc.sbuf_tensor("wab", [128, 4 * DC * HGA], BF16) as wab,
                nc.sbuf_tensor("w3sb", [128, 2 * 4 * 1024], BF16) as w3sb,
                nc.sbuf_tensor("sga", [128, 2 * 512], F32) as sga,
                nc.sbuf_tensor("hb", [128, 4 * 1536], BF16) as hb,
                nc.sbuf_tensor("osb", [128, 12 * 1024], F32) as osb,
                nc.psum_tensor("psa", [128, 2048], F32) as psa,
                nc.psum_tensor("psd", [128, 2048], F32) as psd,
            ):
                pe_marks = []
                dve_marks = {}
                acc_marks = []
                i = 0
                ost_tok = None
                for ho, TH in ((0, 1536), (1536, 1024)):
                    TB = TH // 128
                    SG = TH // 512
                    for hg in range(HM // HGA):
                        par = hg % 2
                        w1v = wab[:, (2 * par) * DC * HGA:(2 * par + 1) * DC * HGA]
                        w2v = wab[:, (2 * par + 1) * DC * HGA:(2 * par + 2) * DC * HGA]
                        w3v = w3sb[:, par * 4 * 1024:(par + 1) * 4 * 1024]
                        prev = pe_marks[-2] if len(pe_marks) >= 2 else None

                        def f_w(g, hg=hg, w1v=w1v, w2v=w2v, w3v=w3v, prev=prev):
                            if prev is not None:
                                g.wait_ge(p.pe.h, prev)
                            for w_d_, dst in ((w1_d, w1v), (w2_d, w2v)):
                                src = w_d_.rearrange("(dc q) h -> q dc h", q=128)[:, :, hg * HGA:(hg + 1) * HGA]
                                p.dg.inc(g.dma_start(out=dst.rearrange("q (dc h) -> q dc h", dc=DC),
                                                     in_=src), 16)
                            src3 = w3_d.rearrange("(hc q) d -> q hc d", q=128)[:, hg * 4:(hg + 1) * 4, :]
                            p.dg.inc(g.dma_start(out=w3v.rearrange("q (hc d) -> q hc d", hc=4),
                                                 in_=src3), 16)
                        p.block.gpsimd(f_w)
                        wn = p.dg.tok()

                        for hc in range(HGA // 128):
                            for sg in range(SG):
                                sub = i % 2
                                pg = psa[:, sub * 1024:sub * 1024 + 512]
                                pu = psa[:, sub * 1024 + 512:sub * 1024 + 1024]
                                emark = dve_marks.get(i - 2)

                                def f_mm(t, w1v=w1v, w2v=w2v, hc=hc, sg=sg, wn=wn,
                                         ho=ho, pg=pg, pu=pu, emark=emark):
                                    p.dg.wait(t, wn)
                                    if emark is not None:
                                        t.wait_ge(p.dve.h, emark)
                                    for wv_, pp in ((w1v, pg), (w2v, pu)):
                                        wr = wv_.rearrange("q (dc h) -> q dc h", dc=DC)
                                        for dc in range(DC):
                                            p.pe.inc(t.matmul(pp, wr[:, dc, hc * 128:(hc + 1) * 128],
                                                     xe[:, dc * CAP + ho + sg * 512: dc * CAP + ho + (sg + 1) * 512],
                                                     start=(dc == 0), stop=(dc == DC - 1)))
                                p.block.tensor(f_mm)
                                mmn = p.pe.n
                                sgv = sga[:, (i % 2) * 512:(i % 2 + 1) * 512]

                                def f_silu(s, pg=pg, mmn=mmn, sgv=sgv, emark=emark):
                                    s.wait_ge(p.pe.h, mmn)
                                    if emark is not None:
                                        s.wait_ge(p.dve.h, emark)
                                    p.act.inc(s.activation(sgv, pg, AF.Silu))
                                p.block.scalar(f_silu)
                                sn = p.act.n
                                hv = hb[:, hc * TH + sg * 512: hc * TH + (sg + 1) * 512]

                                def f_h(v, pu=pu, sgv=sgv, hv=hv, sn=sn):
                                    v.wait_ge(p.act.h, sn)
                                    p.dve.inc(v.scalar_tensor_tensor(hv, pu, 1.0, sgv, ALU.mult, ALU.mult))
                                p.block.vector(f_h)
                                dve_marks[i] = p.dve.n
                                i += 1
                        hrdy = p.dve.n

                        for tb in range(TB):
                            pdv = psd[:, (tb % 2) * 1024:(tb % 2 + 1) * 1024]
                            amark = acc_marks[-2] if len(acc_marks) >= 2 else None

                            def f_dn(t, tb=tb, hg=hg, TH=TH, pdv=pdv, hrdy=hrdy,
                                     w3v=w3v, wn=wn, amark=amark):
                                if tb == 0:
                                    t.wait_ge(p.dve.h, hrdy)
                                    p.dg.wait(t, wn)
                                if amark is not None:
                                    t.wait_ge(p.dve.h, amark)
                                for hc in range(4):
                                    for nh in range(2):
                                        p.pe.inc(t.matmul(pdv[:, nh * 512:(nh + 1) * 512],
                                                 hb[:, hc * TH + tb * 128: hc * TH + (tb + 1) * 128],
                                                 w3v[:, hc * 1024 + nh * 512: hc * 1024 + (nh + 1) * 512],
                                                 start=(hc == 0), stop=(hc == 3)))
                            p.block.tensor(f_dn)
                            dnn = p.pe.n
                            ov = osb[:, tb * 1024:(tb + 1) * 1024]

                            def f_acc(v, pdv=pdv, dnn=dnn, ov=ov, hg=hg,
                                      ost_tok=ost_tok):
                                v.wait_ge(p.pe.h, dnn)
                                if hg == 0:
                                    if ost_tok is not None:
                                        p.dw.wait(v, ost_tok)
                                    p.dve.inc(v.tensor_copy(ov, pdv))
                                else:
                                    p.dve.inc(v.tensor_add(ov, ov, pdv))
                            p.block.vector(f_acc)
                            acc_marks.append(p.dve.n)

                            if hg == HM // HGA - 1:
                                oen = p.dve.n

                                def f_os(sy, ho=ho, tb=tb, ov=ov, oen=oen):
                                    sy.wait_ge(p.dve.h, oen)
                                    p.dw.inc(sy.dma_start(
                                        out=o_d[ho + tb * 128: ho + (tb + 1) * 128, :],
                                        in_=ov), 16)
                                p.block.sync(f_os)
                        pe_marks.append(p.pe.n)
                    ost_tok = p.dw.tok()

                def f_fin(sy):
                    p.dw.wait(sy, p.dw.tok())
                p.block.sync(f_fin)
    return nc


def emit_combine(p, C, Xsb, lg_d, lgown_d, oall_d, ownsel_d):
    nc = p.nc
    rstk = ExitStack()
    r = emit_routing(p, C, lg_d, rstk)
    seqs = r["seqs"]

    with (
        nc.sbuf_tensor(None, [128, 8 * 128], F32, allow_name_mangling=True) as stm,
        nc.sbuf_tensor(None, [128, 8], F32, allow_name_mangling=True) as ownsel,
        nc.sbuf_tensor(None, [128, NCH * 8], F32, allow_name_mangling=True) as sqo,
        nc.sbuf_tensor(None, [128, NCH * 8], F32, allow_name_mangling=True) as lgo,
        nc.sbuf_tensor(None, [128, NCH * 8], F32, allow_name_mangling=True) as wrk,
        nc.sbuf_tensor(None, [128, NCH * 8], F32, allow_name_mangling=True) as wrk2,
        nc.sbuf_tensor(None, [128, 16 * NCH], F32, allow_name_mangling=True) as tok,
        nc.sbuf_tensor(None, [128, 2 * NCH], I32, allow_name_mangling=True) as gidx,
        nc.sbuf_tensor(None, [128, 2 * 1024], F32, allow_name_mangling=True) as ob,
        nc.psum_tensor(None, [128, 1024], F32, allow_name_mangling=True) as ptf,
    ):
        def f_ld(g):
            p.dg.inc(g.dma_start(out=ownsel[:, :], in_=ownsel_d[:, :]), 16)
            p.dg.inc(g.dma_start(out=lgo[:, :], in_=lgown_d[:, :]), 16)
        p.block.gpsimd(f_ld)
        ldn = p.dg.tok()

        tm_marks = {}
        for e in range(8):
            def f_tr(t, e=e):
                if e >= 2:
                    t.wait_ge(p.dve.h, tm_marks[e - 2])
                p.pe.inc(t.transpose(ptf[:, (e % 2) * 512:(e % 2) * 512 + 128],
                            seqs[:, e * 128:(e + 1) * 128], C["id128f"][:, :]))
            p.block.tensor(f_tr)
            tn = p.pe.n

            def f_ev(v, e=e, tn=tn):
                v.wait_ge(p.pe.h, tn)
                dst = stm[:, :].rearrange("q (c e) -> q c e", e=8)[:, :, e]
                p.dve.inc(v.tensor_copy(dst, ptf[:, (e % 2) * 512:(e % 2) * 512 + 128]))
            p.block.vector(f_ev)
            tm_marks[e] = p.dve.n
        smn = p.dve.n

        def f_own0(v, smn=smn, ldn=ldn):
            p.dg.wait(v, ldn)
            v.wait_ge(p.dve.h, smn)
            p.dve.inc(v.memset(sqo[:, :], 0.0))
            for g in range(8):
                v.wait_ge(p.dve.h, p.dve.n)
                p.dve.inc(v.scalar_tensor_tensor(sqo[:, :], stm[:, g * NCH * 8:(g + 1) * NCH * 8],
                                       ownsel[:, g:g + 1], sqo[:, :], ALU.mult, ALU.add))
        p.block.vector(f_own0)

        lgv = lgo[:, :].rearrange("q (c e) -> q c e", e=8)

        def f_own(v):
            m1 = tok[:, 0:NCH]
            m2 = tok[:, NCH:2 * NCH]
            i1 = tok[:, 2 * NCH:3 * NCH]
            i2 = tok[:, 3 * NCH:4 * NCH]
            s1 = tok[:, 4 * NCH:5 * NCH]
            s2 = tok[:, 5 * NCH:6 * NCH]
            dv = tok[:, 8 * NCH:9 * NCH]
            def q(inst):
                p.dve.inc(inst)
                v.wait_ge(p.dve.h, p.dve.n)
            q(v.tensor_reduce(m1, lgv, AX.X, ALU.max))
            m1b = m1.rearrange("q c -> q c ()").broadcast_to([128, NCH, 8])
            q(v.tensor_tensor(wrk.rearrange("q (c e) -> q c e", e=8), lgv, m1b, ALU.is_equal))
            q(v.scalar_tensor_tensor(wrk2[:, :], wrk[:, :], -1e30, lgo[:, :], ALU.mult, ALU.add))
            q(v.tensor_reduce(m2, wrk2.rearrange("q (c e) -> q c e", e=8), AX.X, ALU.max))
            m2b = m2.rearrange("q c -> q c ()").broadcast_to([128, NCH, 8])
            q(v.tensor_tensor(wrk2.rearrange("q (c e) -> q c e", e=8),
                            wrk2.rearrange("q (c e) -> q c e", e=8), m2b, ALU.is_equal))
            q(v.tensor_mul(wrk[:, :], wrk[:, :], C["iota_e"][:, :]))
            q(v.tensor_reduce(i1, wrk.rearrange("q (c e) -> q c e", e=8), AX.X, ALU.add))
            q(v.tensor_tensor(wrk.rearrange("q (c e) -> q c e", e=8), lgv, m1b, ALU.is_equal))
            q(v.tensor_mul(wrk[:, :], wrk[:, :], sqo[:, :]))
            q(v.tensor_reduce(s1, wrk.rearrange("q (c e) -> q c e", e=8), AX.X, ALU.add))
            q(v.tensor_mul(wrk2[:, :], wrk2[:, :], C["iota_e"][:, :]))
            q(v.tensor_reduce(i2, wrk2.rearrange("q (c e) -> q c e", e=8), AX.X, ALU.add))
            q(v.tensor_tensor(wrk.rearrange("q (c e) -> q c e", e=8), lgv, m1b, ALU.is_equal))
            q(v.scalar_tensor_tensor(wrk2[:, :], wrk[:, :], -1e30, lgo[:, :], ALU.mult, ALU.add))
            q(v.tensor_tensor(wrk2.rearrange("q (c e) -> q c e", e=8),
                            wrk2.rearrange("q (c e) -> q c e", e=8), m2b, ALU.is_equal))
            q(v.tensor_mul(wrk2[:, :], wrk2[:, :], sqo[:, :]))
            q(v.tensor_reduce(s2, wrk2.rearrange("q (c e) -> q c e", e=8), AX.X, ALU.add))
            p.dve.inc(v.tensor_sub(dv, m2, m1))
        p.block.vector(f_own)
        on1 = p.dve.n

        def f_sig(s, on1=on1):
            s.wait_ge(p.dve.h, on1)
            p.act.inc(s.activation(tok[:, 9 * NCH:10 * NCH], tok[:, 8 * NCH:9 * NCH], AF.Exp))
        p.block.scalar(f_sig)
        sgn = p.act.n

        def f_own2(v, sgn=sgn):
            v.wait_ge(p.act.h, sgn)
            i1 = tok[:, 2 * NCH:3 * NCH]; i2 = tok[:, 3 * NCH:4 * NCH]
            s1 = tok[:, 4 * NCH:5 * NCH]; s2 = tok[:, 5 * NCH:6 * NCH]
            w1 = tok[:, 6 * NCH:7 * NCH]; w2 = tok[:, 7 * NCH:8 * NCH]
            ex = tok[:, 9 * NCH:10 * NCH]
            g1 = tok[:, 10 * NCH:11 * NCH]; g2 = tok[:, 11 * NCH:12 * NCH]
            vv1 = tok[:, 12 * NCH:13 * NCH]; vv2 = tok[:, 13 * NCH:14 * NCH]
            def q(inst):
                p.dve.inc(inst)
                v.wait_ge(p.dve.h, p.dve.n)
            q(v.tensor_scalar_add(w1, ex, 1.0))
            q(v.reciprocal(w1, w1))
            q(v.tensor_mul(w2, ex, w1))
            q(v.tensor_single_scalar(vv1, s1, float(CAP), ALU.is_le))
            q(v.tensor_single_scalar(vv2, s2, float(CAP), ALU.is_le))
            q(v.tensor_mul(w1, w1, vv1))
            q(v.tensor_mul(w2, w2, vv2))
            q(v.tensor_scalar_mul(w1, w1, 0.5))
            q(v.tensor_scalar_mul(w2, w2, 0.5))
            for gg, ii, ss, vv in ((g1, i1, s1, vv1), (g2, i2, s2, vv2)):
                q(v.tensor_scalar(gg, ii, float(CAP), -1.0, ALU.mult, ALU.add))
                q(v.tensor_add(gg, gg, ss))
                q(v.tensor_scalar_add(gg, gg, -float(ODUMP)))
                q(v.tensor_mul(gg, gg, vv))
                q(v.tensor_scalar_add(gg, gg, float(ODUMP)))
            v.tensor_copy(gidx[:, 0:NCH], tok[:, 10 * NCH:11 * NCH])
            p.dve.inc(v.tensor_copy(gidx[:, NCH:2 * NCH], tok[:, 11 * NCH:12 * NCH]))
        p.block.vector(f_own2)
        on2 = p.dve.n

        add_marks = {}
        for b in range(NCH):
            amark = add_marks.get(b - 1)

            def f_g2(g, b=b, on2=on2, amark=amark):
                g.wait_ge(p.dve.h, on2)
                if amark is not None:
                    g.wait_ge(p.dve.h, amark)
                p.dg.inc(g.indirect_dma_start(
                    ob[:, 0:1024], None, oall_d[0:NCORES * CAP + 1, :],
                    IndirectOffsetOnAxis(gidx[:, b:b + 1], 0)), 16)
                p.dg.inc(g.indirect_dma_start(
                    ob[:, 1024:2048], None, oall_d[0:NCORES * CAP + 1, :],
                    IndirectOffsetOnAxis(gidx[:, NCH + b:NCH + b + 1], 0)), 16)
            p.block.gpsimd(f_g2)
            g2n = p.dg.tok()
            xv = Xsb[:, b * 1024:(b + 1) * 1024]

            def f_add(v, b=b, g2n=g2n, xv=xv):
                p.dg.wait(v, g2n)
                v.scalar_tensor_tensor(xv, ob[:, 0:1024], tok[:, 6 * NCH + b:6 * NCH + b + 1],
                                       xv, ALU.mult, ALU.add)
                p.dve.inc(v.scalar_tensor_tensor(xv, ob[:, 1024:2048], tok[:, 7 * NCH + b:7 * NCH + b + 1],
                                       xv, ALU.mult, ALU.add))
            p.block.vector(f_add)
            add_marks[b] = p.dve.n
    p.fence()
    rstk.close()


def build_l2b(final=False):
    nc = bass.Bass("TRN2", target_bir_lowering=False)
    C = make_consts(nc)
    x_d = nc.dram_tensor("x", [T, D], F32, kind="ExternalInput").ap()
    lg_d = nc.dram_tensor("lga", [128, TCH * 8], F32, kind="ExternalInput").ap()
    lgown_d = nc.dram_tensor("lgown", [128, NCH * 8], F32, kind="ExternalInput").ap()
    oall_d = nc.dram_tensor("oall", [NCORES * CAP + 1, D], F32, kind="ExternalInput").ap()
    ownsel_d = nc.dram_tensor("ownsel", [128, 8], F32, kind="ExternalInput").ap()
    if not final:
        wqkv_d = nc.dram_tensor("wqkv1", [D, 3 * D], F32, kind="ExternalInput").ap()
        wout_d = nc.dram_tensor("wout1", [D, D], F32, kind="ExternalInput").ap()
        bout_d = nc.dram_tensor("bout1", [1, D], F32, kind="ExternalInput").ap()
        wgate_d = nc.dram_tensor("wgate", [D, 8], F32, kind="ExternalInput").ap()
        bg_d = nc.dram_tensor("bg", [1, 8], F32, kind="ExternalInput").ap()
        xo_d = nc.dram_tensor("xo", [T, D], F32, kind="ExternalOutput").ap()
        lnx_d = nc.dram_tensor("lnx", [T, D], BF16, kind="ExternalOutput").ap()
        lgo_d = nc.dram_tensor("lg", [T, 8], F32, kind="ExternalOutput").ap()
    else:
        y_d = nc.dram_tensor("y", [T, D], F32, kind="ExternalOutput").ap()

    with ExitStack() as stack:
        Xsb = stack.enter_context(nc.sbuf_tensor("Xsb", [128, NCH * 1024], F32))
        block = stack.enter_context(nc.Block())
        p = P(nc, block, stack)
        keys = ["id128", "id128f", "iota_e"]
        if not final:
            keys += ["ones_bf", "ones_f32", "qkscale"]
        load_consts(p, C, keys)

        def f_ldx(sy):
            p.dw.inc(sy.dma_start(out=Xsb[:, :].rearrange("q (c d) -> q c d", c=NCH),
                                  in_=x_d.rearrange("(c q) d -> q c d", q=128)), 16)
        p.block.sync(f_ldx)
        p.fence()

        emit_combine(p, C, Xsb[:, :], lg_d, lgown_d, oall_d, ownsel_d)

        if final:
            def f_sty(sy):
                sy.wait_ge(p.dve.h, p.dve.n)
                p.dw.inc(sy.dma_start(out=y_d.rearrange("(c q) d -> q c d", q=128),
                                      in_=Xsb[:, :].rearrange("q (c d) -> q c d", c=NCH)), 16)
                p.dw.wait(sy, p.dw.tok())
            p.block.sync(f_sty)
        else:
            emit_attn(p, C, Xsb[:, :], 1, wqkv_d, wout=wout_d, bout=bout_d)
            with (
                nc.sbuf_tensor("lnxf6", [128, DC * T], BF16) as lnxfm,
                nc.sbuf_tensor("lnxt6", [128, NCH * 1024], BF16) as lnxtm,
                nc.sbuf_tensor("lgsb", [128, NCH * 8], F32) as lgsb,
            ):
                emit_ln(p, C, Xsb[:, :], lnxfm[:, :], 0, NCH, lnxtm=lnxtm[:, :])
                emit_gating(p, C, lnxfm[:, :], wgate_d, bg_d, lgsb[:, :])

                def f_st(sy):
                    p.dw.inc(sy.dma_start(out=xo_d.rearrange("(c q) d -> q c d", q=128),
                                          in_=Xsb[:, :].rearrange("q (c d) -> q c d", c=NCH)), 16)
                    p.dw.inc(sy.dma_start(out=lnx_d.rearrange("(c q) d -> q c d", q=128),
                                          in_=lnxtm[:, :].rearrange("q (c d) -> q c d", c=NCH)), 16)
                    p.dw.inc(sy.dma_start(out=lgo_d.rearrange("(c q) e -> q c e", q=128),
                                          in_=lgsb[:, :].rearrange("q (c e) -> q c e", c=NCH)), 16)
                    p.dw.wait(sy, p.dw.tok())
                p.block.sync(f_st)
    return nc



_PROGS = {}


def _progs():
    if not _PROGS:
        _PROGS["l1"] = build_l1()
        _PROGS["l2a"] = build_l2a()
        _PROGS["l2b"] = build_l2b(final=False)
        _PROGS["l3b"] = build_l2b(final=True)
    return _PROGS


def kernel(L0_attn_x, L0_attn_q_w, L0_attn_q_b, L0_attn_k_w, L0_attn_k_b,
           L0_attn_v_w, L0_attn_v_b, L0_attn_l_w, L0_attn_l_b,
           L0_ffn_w_gate, L0_ffn_w_up, L0_ffn_w_down,
           L1_attn_w_qkv, L1_attn_w_out, L1_attn_b_out,
           L1_ffn_Wg, L1_ffn_bg, L1_ffn_W1, L1_ffn_W2, L1_ffn_W3):
    pr = _progs()
    f32 = np.float32
    cc = np.ascontiguousarray

    x = np.asarray(L0_attn_x, f32)
    wg_t = cc(np.asarray(L0_ffn_w_gate).T)
    wu_t = cc(np.asarray(L0_ffn_w_up).T)
    wd_t = cc(np.asarray(L0_ffn_w_down).T)
    qT = np.asarray(L0_attn_q_w, f32).transpose(2, 0, 1).reshape(D, D)
    kT = np.asarray(L0_attn_k_w, f32).transpose(2, 0, 1).reshape(D, D)
    vT = np.asarray(L0_attn_v_w, f32).transpose(2, 0, 1).reshape(D, D)
    wqkv0 = cc(np.concatenate([qT, kT, vT], axis=1))
    qkb0 = np.zeros((128, 16), f32)
    qkb0[:64, :] = np.asarray(L0_attn_q_b, f32).reshape(16, 64).T
    qkb0[64:, :] = np.asarray(L0_attn_k_b, f32).reshape(16, 64).T
    vb0 = np.asarray(L0_attn_v_b, f32).reshape(1, D)
    lw_t = cc(np.asarray(L0_attn_l_w, f32).T)
    lb0 = np.asarray(L0_attn_l_b, f32).reshape(1, D)
    wqkv1 = cc(np.asarray(L1_attn_w_qkv, f32))
    wout1 = cc(np.asarray(L1_attn_w_out, f32))
    bout1 = np.asarray(L1_attn_b_out, f32).reshape(1, D)
    wgate = cc(np.asarray(L1_ffn_Wg, f32))
    bgv = np.asarray(L1_ffn_bg, f32).reshape(1, 8)
    W1 = np.asarray(L1_ffn_W1, f32)
    W2 = np.asarray(L1_ffn_W2, f32)
    W3 = np.asarray(L1_ffn_W3, f32)

    core_ids = list(range(NCORES))

    maps = []
    for c in range(NCORES):
        maps.append({"x": cc(x[c]), "wg_t": wg_t, "wu_t": wu_t, "wd_t": wd_t,
                     "wqkv0": wqkv0, "qkb0": qkb0, "vb0": vb0, "lw_t": lw_t,
                     "lb0": lb0, "wgate": wgate, "bg": bgv})
    r1 = run_bass_kernel_spmd(pr["l1"], maps, core_ids=core_ids)
    x4pre = [np.asarray(r1.results[c]["xo"]) for c in range(NCORES)]
    lnx4 = [np.asarray(r1.results[c]["lnx"]).view(bfd) for c in range(NCORES)]
    lg4 = [np.asarray(r1.results[c]["lg"]) for c in range(NCORES)]

    def rearr(lg_list):
        a = np.concatenate(lg_list, axis=0)
        return cc(a.reshape(TCH, 128, 8).swapaxes(0, 1).reshape(128, TCH * 8))

    def rearr_own(a):
        return cc(a.reshape(NCH, 128, 8).swapaxes(0, 1).reshape(128, NCH * 8))

    lg4_all = rearr(lg4)
    lnx4_all = cc(np.concatenate(lnx4 + [np.zeros((1, D), bfd)], axis=0))
    esels = []
    for c in range(NCORES):
        e1 = np.zeros((128, 8), f32)
        e1[:, c] = 1.0
        esels.append(e1)

    maps = [{"lga": lg4_all, "lnxa": lnx4_all, "w1": cc(W1[c]), "w2": cc(W2[c]),
             "w3": cc(W3[c]), "esel": esels[c]} for c in range(NCORES)]
    r2a = run_bass_kernel_spmd(pr["l2a"], maps, core_ids=core_ids)
    o4_all = cc(np.concatenate([np.asarray(r2a.results[c]["o"]) for c in range(NCORES)]
                               + [np.zeros((1, D), f32)], axis=0))

    maps = [{"x": x4pre[c], "lga": lg4_all, "lgown": rearr_own(lg4[c]), "oall": o4_all,
             "ownsel": esels[c], "wqkv1": wqkv1, "wout1": wout1, "bout1": bout1,
             "wgate": wgate, "bg": bgv} for c in range(NCORES)]
    r2b = run_bass_kernel_spmd(pr["l2b"], maps, core_ids=core_ids)
    x6pre = [np.asarray(r2b.results[c]["xo"]) for c in range(NCORES)]
    lnx6 = [np.asarray(r2b.results[c]["lnx"]).view(bfd) for c in range(NCORES)]
    lg6 = [np.asarray(r2b.results[c]["lg"]) for c in range(NCORES)]

    lg6_all = rearr(lg6)
    lnx6_all = cc(np.concatenate(lnx6 + [np.zeros((1, D), bfd)], axis=0))

    maps = [{"lga": lg6_all, "lnxa": lnx6_all, "w1": cc(W1[c]), "w2": cc(W2[c]),
             "w3": cc(W3[c]), "esel": esels[c]} for c in range(NCORES)]
    r3a = run_bass_kernel_spmd(pr["l2a"], maps, core_ids=core_ids)
    o6_all = cc(np.concatenate([np.asarray(r3a.results[c]["o"]) for c in range(NCORES)]
                               + [np.zeros((1, D), f32)], axis=0))

    maps = [{"x": x6pre[c], "lga": lg6_all, "lgown": rearr_own(lg6[c]), "oall": o6_all,
             "ownsel": esels[c]} for c in range(NCORES)]
    r3b = run_bass_kernel_spmd(pr["l3b"], maps, core_ids=core_ids)
    y = np.stack([np.asarray(r3b.results[c]["y"]) for c in range(NCORES)], axis=0)
    return y.astype(np.float32)


# version 05
# speedup vs baseline: 1.1916x; 1.1916x over previous
from contextlib import ExitStack

import numpy as np
import ml_dtypes
import concourse.bass as bass
import concourse.mybir as mybir
from concourse.bass_utils import run_bass_kernel_spmd
from concourse.bass_core_rust import IndirectOffsetOnAxis

F32 = mybir.dt.float32
BF16 = mybir.dt.bfloat16
I32 = mybir.dt.int32
AF = mybir.ActivationFunctionType
ALU = mybir.AluOpType
AX = mybir.AxisListType
bfd = ml_dtypes.bfloat16

NCORES = 8
T = 2048
NCH = 16
D = 1024
DC = 8
H0 = 3072
HM = 4096
CAP = 2560
NSB = CAP // 128
HEADS = 16
EPS = 1e-5
QDEQ = float(np.float32(1.0) / np.float32(127.0))
TALL = NCORES * T
TCH = TALL // 128
VW = 65
SENT = TALL
ODUMP = NCORES * CAP


class Ctr:
    def __init__(self, stack, nc, name):
        self.h = stack.enter_context(nc.semaphore(name))
        self.n = 0

    def inc(self, inst, n=1):
        inst.then_inc(self.h, n)
        self.n += n
        return inst

    def tok(self):
        return (("c", self.h, self.n),)

    def wait(self, eng, tok):
        for kind, h, v in tok:
            if v > 0:
                eng.wait_ge(h, v)


class DmaQ:

    NSEM = 8

    def __init__(self, stack, nc, name):
        self.sems = [stack.enter_context(nc.semaphore(f"{name}{i}"))
                     for i in range(self.NSEM)]
        self.cnt = [0] * self.NSEM
        self.cur = 0

    def inc(self, inst, n=16):
        i = self.cur
        self.cur = (i + 1) % self.NSEM
        if self.cnt[i] > 0:
            inst._wait_ge(self.sems[i], self.cnt[i])
        inst.then_inc(self.sems[i], n)
        self.cnt[i] += n
        return inst

    def tok(self):
        return tuple(("q", self.sems[i], self.cnt[i]) for i in range(self.NSEM)
                     if self.cnt[i] > 0)

    def wait(self, eng, tok):
        for kind, h, v in tok:
            eng.wait_ge(h, v)


class P:
    def __init__(self, nc, block, stack):
        self.nc = nc
        self.block = block
        self.stack = stack
        self.dw = DmaQ(stack, nc, "s_dw")
        self.dg = DmaQ(stack, nc, "s_dg")
        self.pe = Ctr(stack, nc, "s_pe")
        self.act = Ctr(stack, nc, "s_act")
        self.dve = Ctr(stack, nc, "s_dve")

    def fence(self):
        toks = self.dw.tok() + self.dg.tok() + self.pe.tok() + self.act.tok() + self.dve.tok()
        own = {"tensor": self.pe.h, "vector": self.dve.h, "scalar": self.act.h,
               "gpsimd": None, "sync": None}
        for ename, sec in (("tensor", self.block.tensor), ("vector", self.block.vector),
                           ("scalar", self.block.scalar), ("gpsimd", self.block.gpsimd),
                           ("sync", self.block.sync)):
            skip = own[ename]

            def f(e, toks=toks, skip=skip):
                for kind, h, v in toks:
                    if h is not skip and v > 0:
                        e.wait_ge(h, v)
            sec(f)


def load_consts(p, C, keys):
    nc = p.nc
    for k in keys:
        d = C[k]
        t = p.stack.enter_context(
            nc.sbuf_tensor("cs_" + k, list(d.shape), d.dtype))

        def f(sy, t=t, d=d):
            p.dw.inc(sy.dma_start(out=t[:, :], in_=d[:, :]), 16)
        p.block.sync(f)
        C[k] = t[:, :]
    p.fence()


def make_consts(nc):
    c = {}
    c["id128"] = nc.inline_tensor(np.eye(128, dtype=bfd), "c_id128").ap()
    c["id128f"] = nc.inline_tensor(np.eye(128, dtype=np.float32), "c_id128f").ap()
    c["ones_bf"] = nc.inline_tensor(np.ones((128, 128), dtype=bfd), "c_ones_bf").ap()
    c["ones_f32"] = nc.inline_tensor(np.ones((1, 128), dtype=np.float32), "c_ones_f32").ap()
    qs = np.ones((128, 1), np.float32)
    qs[:64] = 0.125
    c["qkscale"] = nc.inline_tensor(qs, "c_qkscale").ap()
    half = 32
    theta = 1.0 / 10000.0 ** (np.arange(half, dtype=np.float32) / half)
    ang = np.arange(T, dtype=np.float32)[None, :] * theta[:, None]
    cosd = np.concatenate([np.cos(ang), np.cos(ang)], 0)
    sind = np.concatenate([-np.sin(ang), np.sin(ang)], 0)
    c["ropecos"] = nc.inline_tensor(cosd.astype(bfd), "c_ropecos").ap()
    c["ropesin"] = nc.inline_tensor(sind.astype(bfd), "c_ropesin").ap()
    pm = np.zeros((64, 64), dtype=bfd)
    pm[np.arange(64), (np.arange(64) + 32) % 64] = 1
    c["ropepm"] = nc.inline_tensor(pm, "c_ropepm").ap()
    tid = (np.arange(128)[:, None] * 128 + np.arange(128)[None, :]).astype(np.int32)
    tid2 = np.repeat(tid[:, :, None], 2, axis=2).reshape(128, 256)
    c["tid2"] = nc.inline_tensor(tid2, "c_tid2").ap()
    c["iota_e"] = nc.inline_tensor(
        np.broadcast_to(np.arange(8, dtype=np.float32), (128, NCH, 8)).reshape(128, NCH * 8).copy(),
        "c_iota_e").ap()
    return c



def emit_ln(p, C, Xsb, lnxfm, c0, c1, lnxtm=None):
    nc = p.nc
    ntok = (c1 - c0) * 128
    with (
        nc.sbuf_tensor(None, [128, 4 * NCH], F32, allow_name_mangling=True) as st,
        nc.sbuf_tensor(None, [128, 1024], F32, allow_name_mangling=True) as sq,
        nc.sbuf_tensor(None, [128, 2 * 1024], BF16, allow_name_mangling=True) as tmbuf,
        nc.psum_tensor(None, [128, 2048], BF16, allow_name_mangling=True) as ptr,
    ):
        sums = st[:, 0:NCH]
        sqs = st[:, NCH:2 * NCH]
        rstd = st[:, 2 * NCH:3 * NCH]
        nbias = st[:, 3 * NCH:4 * NCH]
        sl = slice(c0, c1)

        def f_sum(v):
            for c in range(c0, c1):
                p.dve.inc(v.tensor_reduce(sums[:, c:c + 1], Xsb[:, c * 1024:(c + 1) * 1024], AX.X, ALU.add))
        p.block.vector(f_sum)

        def f_sq(s):
            for c in range(c0, c1):
                p.act.inc(s.activation(sq[:, :], Xsb[:, c * 1024:(c + 1) * 1024], AF.Square,
                             accum_out=sqs[:, c:c + 1]))
        p.block.scalar(f_sq)
        a0 = p.act.n

        def f_stats(v):
            v.wait_ge(p.act.h, a0)
            v.wait_ge(p.dve.h, p.dve.n)
            p.dve.inc(v.tensor_scalar_mul(sums[:, sl], sums[:, sl], 1.0 / 1024.0))
            v.wait_ge(p.dve.h, p.dve.n)
            p.dve.inc(v.tensor_scalar_mul(sqs[:, sl], sqs[:, sl], 1.0 / 1024.0))
            v.wait_ge(p.dve.h, p.dve.n)
            p.dve.inc(v.scalar_tensor_tensor(rstd[:, sl], sums[:, sl], -1.0, sums[:, sl], ALU.mult, ALU.mult))
            v.wait_ge(p.dve.h, p.dve.n)
            p.dve.inc(v.tensor_add(rstd[:, sl], rstd[:, sl], sqs[:, sl]))
            v.wait_ge(p.dve.h, p.dve.n)
            p.dve.inc(v.tensor_scalar_add(rstd[:, sl], rstd[:, sl], EPS))
        p.block.vector(f_stats)
        v1 = p.dve.n

        def f_rstd(s):
            s.wait_ge(p.dve.h, v1)
            s.activation(rstd[:, sl], rstd[:, sl], AF.Ln)
            p.act.inc(s.activation(rstd[:, sl], rstd[:, sl], AF.Exp, scale=-0.5))
        p.block.scalar(f_rstd)
        a1 = p.act.n

        def f_nb(v):
            v.wait_ge(p.act.h, a1)
            p.dve.inc(v.tensor_scalar_mul(nbias[:, sl], sums[:, sl], -1.0))
        p.block.vector(f_nb)
        v2 = p.dve.n

        pe_marks, dve_marks = {}, {}
        for c in range(c0, c1):
            ci = c - c0
            par = ci % 2
            tmv = lnxtm[:, ci * 1024:(ci + 1) * 1024] if lnxtm is not None \
                else tmbuf[:, par * 1024:(par + 1) * 1024]

            def f_norm(v, c=c, ci=ci, tmv=tmv):
                if ci == 0:
                    v.wait_ge(p.dve.h, v2)
                if lnxtm is None and ci >= 2:
                    v.wait_ge(p.pe.h, pe_marks[ci - 2])
                p.dve.inc(v.tensor_scalar(tmv, Xsb[:, c * 1024:(c + 1) * 1024],
                                          nbias[:, c:c + 1], rstd[:, c:c + 1],
                                          ALU.add, ALU.mult))
            p.block.vector(f_norm)
            an = p.dve.n
            ptv = ptr[:, par * 1024:(par + 1) * 1024]

            def f_tr(t, ci=ci, tmv=tmv, ptv=ptv, an=an):
                t.wait_ge(p.dve.h, an)
                if ci >= 2:
                    t.wait_ge(p.dve.h, dve_marks[ci - 2])
                for dc in range(DC):
                    p.pe.inc(t.transpose(ptv[:, dc * 128:(dc + 1) * 128],
                                tmv[:, dc * 128:(dc + 1) * 128], C["id128"][:, :]))
            p.block.tensor(f_tr)
            pe_marks[ci] = p.pe.n
            pn = p.pe.n

            def f_ev(v, ci=ci, ptv=ptv, pn=pn):
                v.wait_ge(p.pe.h, pn)
                for dc in range(DC):
                    p.dve.inc(v.tensor_copy(lnxfm[:, dc * ntok + ci * 128: dc * ntok + (ci + 1) * 128],
                                  ptv[:, dc * 128:(dc + 1) * 128]))
            p.block.vector(f_ev)
            dve_marks[ci] = p.dve.n
    p.fence()



def emit_ffn0(p, C, Xsb, wg_t, wu_t, wd_t):
    nc = p.nc
    TG = 1024
    HG = 512
    for tg in range(T // TG):
        c0, c1 = tg * (TG // 128), (tg + 1) * (TG // 128)
        with (
            nc.sbuf_tensor(None, [128, (H0 // 128) * TG], BF16, allow_name_mangling=True) as hsb,
            nc.sbuf_tensor(None, [128, 2 * 512], F32, allow_name_mangling=True) as sgb,
        ):
            with (
                nc.sbuf_tensor(None, [128, DC * TG], BF16, allow_name_mangling=True) as lnxfm,
                nc.sbuf_tensor(None, [128, 4 * DC * HG], BF16, allow_name_mangling=True) as wbuf,
            ):
                emit_ln(p, C, Xsb, lnxfm, c0, c1)
                with nc.psum_tensor(None, [128, 2048], F32, allow_name_mangling=True) as ps:
                    pe_marks, ev_marks = {}, {}
                    for hg in range(H0 // HG):
                        par = hg % 2
                        wgv = wbuf[:, (2 * par) * DC * HG:(2 * par + 1) * DC * HG]
                        wuv = wbuf[:, (2 * par + 1) * DC * HG:(2 * par + 2) * DC * HG]
                        prev = pe_marks.get(hg - 2)

                        def f_w(g, hg=hg, wgv=wgv, wuv=wuv, prev=prev):
                            if prev is not None:
                                g.wait_ge(p.pe.h, prev)
                            for w_t, dst in ((wg_t, wgv), (wu_t, wuv)):
                                src = w_t.rearrange("(dc q) h -> q dc h", q=128)[:, :, hg * HG:(hg + 1) * HG]
                                p.dg.inc(g.dma_start(
                                    out=dst.rearrange("q (dc h) -> q dc h", dc=DC), in_=src), 16)
                        p.block.gpsimd(f_w)
                        wn = p.dg.tok()

                        for hc in range(HG // 128):
                            hcabs = hg * (HG // 128) + hc
                            for tt in range(TG // 512):
                                i = hcabs * (TG // 512) + tt
                                sub = i % 2
                                pg = ps[:, sub * 1024: sub * 1024 + 512]
                                pu = ps[:, sub * 1024 + 512: sub * 1024 + 1024]
                                emark = ev_marks.get(i - 2)

                                def f_mm(t, wgv=wgv, wuv=wuv, hc=hc, tt=tt, wn=wn,
                                         pg=pg, pu=pu, emark=emark):
                                    p.dg.wait(t, wn)
                                    if emark is not None:
                                        t.wait_ge(p.dve.h, emark)
                                    for wv_, pp in ((wgv, pg), (wuv, pu)):
                                        wr = wv_.rearrange("q (dc h) -> q dc h", dc=DC)
                                        for dc in range(DC):
                                            p.pe.inc(t.matmul(pp, wr[:, dc, hc * 128:(hc + 1) * 128],
                                                     lnxfm[:, dc * TG + tt * 512: dc * TG + (tt + 1) * 512],
                                                     start=(dc == 0), stop=(dc == DC - 1)))
                                p.block.tensor(f_mm)
                                mmn = p.pe.n
                                sgv = sgb[:, (i % 2) * 512:(i % 2 + 1) * 512]

                                def f_silu(s, pg=pg, mmn=mmn, sgv=sgv, emark=emark):
                                    s.wait_ge(p.pe.h, mmn)
                                    if emark is not None:
                                        s.wait_ge(p.dve.h, emark)
                                    p.act.inc(s.activation(sgv, pg, AF.Silu, scale=QDEQ))
                                p.block.scalar(f_silu)
                                sn = p.act.n
                                hv = hsb[:, hcabs * TG + tt * 512: hcabs * TG + (tt + 1) * 512]

                                def f_h(v, pu=pu, sgv=sgv, hv=hv, sn=sn):
                                    v.wait_ge(p.act.h, sn)
                                    p.dve.inc(v.scalar_tensor_tensor(hv, pu, QDEQ, sgv, ALU.mult, ALU.mult))
                                p.block.vector(f_h)
                                ev_marks[i] = p.dve.n
                        pe_marks[hg] = p.pe.n
                p.fence()

            with (
                nc.sbuf_tensor(None, [128, (H0 // 128) * 1024], BF16, allow_name_mangling=True) as wdb,
                nc.psum_tensor(None, [128, 2048], F32, allow_name_mangling=True) as pd,
            ):
                def f_wd(g):
                    src = wd_t.rearrange("(hc q) d -> q hc d", q=128)
                    p.dg.inc(g.dma_start(out=wdb.rearrange("q (hc d) -> q hc d", hc=H0 // 128),
                                         in_=src), 16)
                p.block.gpsimd(f_wd)
                wdn = p.dg.tok()
                dn_marks = {}
                for tc in range(TG // 128):
                    cabs = tg * (TG // 128) + tc
                    par = tc % 2
                    pdv = pd[:, par * 1024:(par + 1) * 1024]
                    emark = dn_marks.get(tc - 2)

                    def f_dn(t, pdv=pdv, tc=tc, wdn=wdn, emark=emark):
                        p.dg.wait(t, wdn)
                        if emark is not None:
                            t.wait_ge(p.dve.h, emark)
                        for hcabs in range(H0 // 128):
                            for nh in range(2):
                                p.pe.inc(t.matmul(pdv[:, nh * 512:(nh + 1) * 512],
                                         hsb[:, hcabs * TG + tc * 128: hcabs * TG + (tc + 1) * 128],
                                         wdb[:, hcabs * 1024 + nh * 512: hcabs * 1024 + (nh + 1) * 512],
                                         start=(hcabs == 0), stop=(hcabs == H0 // 128 - 1)))
                    p.block.tensor(f_dn)
                    dn = p.pe.n
                    xv = Xsb[:, cabs * 1024:(cabs + 1) * 1024]

                    def f_add(v, pdv=pdv, dn=dn, xv=xv):
                        v.wait_ge(p.pe.h, dn)
                        p.dve.inc(v.scalar_tensor_tensor(xv, pdv, 0.5 * QDEQ, xv, ALU.mult, ALU.add))
                    p.block.vector(f_add)
                    dn_marks[tc] = p.dve.n
            p.fence()


def _l0_scores(p, C, vtm, attnb, vpb, zcol, pa, qt_t, kt_t, hi, h, cbev, negC):
    exp_marks, ctx_marks = {}, {}
    for kc in range(NCH):
        av = attnb[:, (kc % 2) * 2048:(kc % 2 + 1) * 2048]
        emark = exp_marks.get(kc - 1)

        def f_smm(t, kc=kc, emark=emark):
            if kc == 0:
                t.wait_ge(p.dve.h, cbev)
            if emark is not None:
                t.wait_ge(p.act.h, emark)
            for qt in range(4):
                p.pe.inc(t.matmul(pa[:, qt * 512:(qt + 1) * 512],
                         kt_t[:, hi * T + kc * 128: hi * T + (kc + 1) * 128],
                         qt_t[:, hi * T + qt * 512: hi * T + (qt + 1) * 512],
                         start=True, stop=True))
        p.block.tensor(f_smm)
        smn = p.pe.n
        cmark = ctx_marks.get(kc - 2)

        def f_exp(s, kc=kc, smn=smn, av=av, cmark=cmark):
            s.wait_ge(p.pe.h, smn)
            if cmark is not None:
                s.wait_ge(p.pe.h, cmark)
            p.act.inc(s.activation(av, pa[:, 0:2048], AF.Exp, bias=negC,
                                   accum_out=zcol[:, kc % 2:kc % 2 + 1]))
        p.block.scalar(f_exp)
        exp_marks[kc] = p.act.n
        en = p.act.n
        vmark = ctx_marks.get(kc - 2)

        def f_vp(v, kc=kc, en=en, vmark=vmark):
            v.wait_ge(p.act.h, en)
            if vmark is not None:
                v.wait_ge(p.pe.h, vmark)
            p.dve.inc(v.reciprocal(zcol[:, 2:3], zcol[:, kc % 2:kc % 2 + 1]))
            v.wait_ge(p.dve.h, p.dve.n)
            vsl = vtm[:, kc * HEADS * VW + h * VW: kc * HEADS * VW + h * VW + 64]
            p.dve.inc(v.tensor_scalar(vpb[:, (kc % 2) * 64:(kc % 2 + 1) * 64],
                            vsl, zcol[:, 2:3], None, ALU.mult))
        p.block.vector(f_vp)
        vpn = p.dve.n

        def f_ctx(t, kc=kc, vpn=vpn, av=av):
            t.wait_ge(p.dve.h, vpn)
            for qt in range(4):
                p.pe.inc(t.matmul(pa[hi * 64:(hi + 1) * 64, 2048 + qt * 512:2048 + (qt + 1) * 512],
                         vpb[:, (kc % 2) * 64:(kc % 2 + 1) * 64],
                         av[:, qt * 512:(qt + 1) * 512],
                         start=(kc == 0), stop=(kc == NCH - 1)))
        p.block.tensor(f_ctx)
        ctx_marks[kc] = p.pe.n


def _l1_headhalf(p, C, vtm, attnb, rzb, zbc, ctmp, ctxfm, pa,
                 qt_t, kt_t, pr, hi, h, qh, cbev, qkready, negC, cdm):
    exp_marks, ctx_marks = {}, {}
    zprev = p.dve.n
    for kce in range(NCH // 2):
        kc0, kc1 = 2 * kce, 2 * kce + 1
        emark = exp_marks.get(kce - 1)

        def f_smm(t, kce=kce, kc0=kc0, kc1=kc1, emark=emark):
            if kce == 0:
                t.wait_ge(p.dve.h, cbev)
                t.wait_ge(p.dve.h, zprev)
                if qkready[0] == "dve":
                    t.wait_ge(p.dve.h, qkready[1])
            if emark is not None:
                t.wait_ge(p.act.h, emark)
            for ki, kc in ((0, kc0), (1, kc1)):
                for qt in range(2):
                    p.pe.inc(t.matmul(pa[:, ki * 1024 + qt * 512: ki * 1024 + (qt + 1) * 512],
                             kt_t[:, hi * T + kc * 128: hi * T + (kc + 1) * 128],
                             qt_t[:, hi * T + qh * 1024 + qt * 512: hi * T + qh * 1024 + (qt + 1) * 512],
                             start=True, stop=True))
        p.block.tensor(f_smm)
        smn = p.pe.n
        cmark = ctx_marks.get(kce - 1)

        def f_exp(s, smn=smn, cmark=cmark):
            s.wait_ge(p.pe.h, smn)
            if cmark is not None:
                s.wait_ge(p.pe.h, cmark)
            p.act.inc(s.activation(attnb[:, 0:2048], pa[:, 0:2048], AF.Exp, bias=negC))
        p.block.scalar(f_exp)
        exp_marks[kce] = p.act.n
        en = p.act.n

        def f_ctx(t, kce=kce, kc0=kc0, kc1=kc1, en=en):
            t.wait_ge(p.act.h, en)
            if kce == 0:
                t.wait_ge(p.dve.h, zprev)
            for ki, kc in ((0, kc0), (1, kc1)):
                vsl = vtm[:, kc * HEADS * VW + h * VW: kc * HEADS * VW + (h + 1) * VW]
                for qt in range(2):
                    p.pe.inc(t.matmul(pa[0:VW, 2048 + qt * 512:2048 + (qt + 1) * 512],
                             vsl, attnb[:, ki * 1024 + qt * 512: ki * 1024 + (qt + 1) * 512],
                             start=(kc == 0), stop=(kc == NCH - 1)))
        p.block.tensor(f_ctx)
        ctx_marks[kce] = p.pe.n
    ctxn = p.pe.n

    def f_z2(s, ctxn=ctxn):
        s.wait_ge(p.pe.h, ctxn)
        s.activation(pa[64:65, 3072:4096], pa[64:65, 2048:3072], AF.Ln)
        p.act.inc(s.activation(rzb[64:65, 0:1024], pa[64:65, 3072:4096], AF.Exp, scale=-1.0))
    p.block.scalar(f_z2)
    z2 = p.act.n

    def f_z3(t, z2=z2):
        t.wait_ge(p.act.h, z2)
        for qt in range(2):
            p.pe.inc(t.matmul(pa[0:64, 3072 + qt * 512:3072 + (qt + 1) * 512],
                     C["ones_bf"][64:65, 0:64],
                     rzb[64:65, qt * 512:(qt + 1) * 512], start=True, stop=True))
    p.block.tensor(f_z3)
    z3 = p.pe.n

    def f_z4(v, z3=z3, cdm=cdm):
        v.wait_ge(p.pe.h, z3)
        v.tensor_copy(zbc[:, :], pa[0:64, 3072:4096])
        dst = (ctxfm[0:64, pr * T + qh * 1024: pr * T + (qh + 1) * 1024]
               if hi == 0 else ctmp[:, :])
        if hi == 1 and cdm is not None:
            p.dw.wait(v, cdm)
        p.dve.inc(v.tensor_tensor(dst, pa[0:64, 2048:3072], zbc[:, :], ALU.mult))
    p.block.vector(f_z4)
    z4 = p.dve.n

    if hi == 1:
        def f_cdma(sy, z4=z4):
            sy.wait_ge(p.dve.h, z4)
            p.dw.inc(sy.dma_start(
                out=ctxfm[64:128, pr * T + qh * 1024: pr * T + (qh + 1) * 1024],
                in_=ctmp[:, :]), 16)
        p.block.sync(f_cdma)
        return p.dw.tok()
    return None



def emit_attn(p, C, Xsb, layer, wqkv, qkb=None, vb=None, wout=None, bout=None):
    nc = p.nc
    with (
        nc.sbuf_tensor(None, [128, DC * T], BF16, allow_name_mangling=True) as lnxfm,
        nc.sbuf_tensor(None, [128, NCH * HEADS * VW], BF16, allow_name_mangling=True) as vtm,
        nc.sbuf_tensor(None, [128, 8 * T], BF16, allow_name_mangling=True) as ctxfm,
    ):
        emit_ln(p, C, Xsb, lnxfm, 0, NCH)

        with (
            nc.sbuf_tensor(None, [128, DC * 1024], BF16, allow_name_mangling=True) as wv,
            nc.sbuf_tensor(None, [1, 1024], BF16, allow_name_mangling=True) as vbs,
            nc.psum_tensor(None, [128, 2048], F32, allow_name_mangling=True) as pv,
        ):
            def f_wv(g):
                src = wqkv.rearrange("(dc q) m -> q dc m", q=128)[:, :, 2048:3072]
                p.dg.inc(g.dma_start(out=wv.rearrange("q (dc m) -> q dc m", dc=DC), in_=src), 16)
                if vb is not None:
                    p.dg.inc(g.dma_start(out=vbs[:, :], in_=vb[:, :]), 16)
            p.block.gpsimd(f_wv)
            wvn = p.dg.tok()

            def f_ones(v):
                ov = vtm[:, :].rearrange("q (c h w) -> q c h w", h=HEADS, w=VW)[:, :, :, 64:65]
                p.dve.inc(v.memset(ov, 1.0))
            p.block.vector(f_ones)

            v_marks = {}
            for sc in range(NCH):
                par = sc % 2
                pvv = pv[:, par * 1024:(par + 1) * 1024]
                emark = v_marks.get(sc - 2)

                def f_vmm(t, sc=sc, pvv=pvv, wvn=wvn, emark=emark):
                    p.dg.wait(t, wvn)
                    if emark is not None:
                        t.wait_ge(p.act.h, emark)
                    for nh in range(2):
                        last = None
                        for dc in range(DC):
                            last = t.matmul(pvv[:, nh * 512:(nh + 1) * 512],
                                            lnxfm[:, dc * T + sc * 128: dc * T + (sc + 1) * 128],
                                            wv[:, dc * 1024 + nh * 512: dc * 1024 + (nh + 1) * 512],
                                            start=(dc == 0), stop=(dc == DC - 1 and vb is None))
                        if vb is not None:
                            last = t.matmul(pvv[:, nh * 512:(nh + 1) * 512], C["ones_bf"][0:1, 0:128],
                                            vbs[0:1, nh * 512:(nh + 1) * 512], start=False, stop=True)
                        p.pe.inc(last)
                p.block.tensor(f_vmm)
                vn = p.pe.n
                dstv = vtm[:, :].rearrange("q (c h w) -> q c h w", h=HEADS, w=VW)[:, sc, :, 0:64]

                def f_vev(s, pvv=pvv, vn=vn, dstv=dstv):
                    s.wait_ge(p.pe.h, vn)
                    p.act.inc(s.activation(dstv, pvv.rearrange("q (h w) -> q h w", w=64), AF.Copy))
                p.block.scalar(f_vev)
                v_marks[sc] = p.act.n
        p.fence()

        with ExitStack() as es:
            _n = [0]

            def _t(shape, dt):
                _n[0] += 1
                return es.enter_context(
                    nc.sbuf_tensor("apx%d" % _n[0], shape, dt, allow_name_mangling=True))
            wqk = _t([128, 2 * 2 * DC * 128], BF16)
            qkbs = _t([128, 16], F32)
            qtb = _t([64, 2 * T], BF16)
            ktb = _t([64, 2 * T], BF16)
            ktmp = _t([128, 2 * 512], BF16)
            attnb = _t([128, (4096 if layer == 0 else 2048)], BF16)
            sqb = attnb[0:64, 0:2048]
            vpb = _t([128, 2 * 64], BF16)
            zcol = _t([128, 8], F32)
            rzb = _t([128, (1024 if layer == 1 else 1)], BF16)
            zbc = _t([64, (1024 if layer == 1 else 1)], BF16)
            ctmp = _t([64, (1024 if layer == 1 else 1)], BF16)
            sc1 = _t([1, 8], F32)
            cosb = _t([64, (2048 if layer == 1 else 1)], BF16)
            sinb = _t([64, (2048 if layer == 1 else 1)], BF16)
            pmb = _t([64, (64 if layer == 1 else 1)], BF16)
            pa = es.enter_context(
                nc.psum_tensor("apx_pa", [128, 4096], F32, allow_name_mangling=True))
            if layer == 1:
                def f_rt(sy):
                    p.dw.inc(sy.dma_start(out=cosb[:, :], in_=C["ropecos"][:, :]), 16)
                    p.dw.inc(sy.dma_start(out=sinb[:, :], in_=C["ropesin"][:, :]), 16)
                    p.dw.inc(sy.dma_start(out=pmb[:, :], in_=C["ropepm"][:, :]), 16)
                p.block.sync(f_rt)
            if qkb is not None:
                def f_qkb(g):
                    p.dg.inc(g.dma_start(out=qkbs[:, :], in_=qkb[:, :]), 16)
                p.block.gpsimd(f_qkb)
                bn = p.dg.tok()

                def f_qkbs(v):
                    p.dg.wait(v, bn)
                    p.dve.inc(v.tensor_tensor(qkbs[:, :], qkbs[:, :],
                                    C["qkscale"][:, :].broadcast_to([128, 16]), ALU.mult))
                    v.wait_ge(p.dve.h, p.dve.n)
                p.block.vector(f_qkbs)

            pair_done = {}
            for pr in range(8):
                par = pr % 2
                wv_ = wqk[:, par * 2 * DC * 128:(par + 1) * 2 * DC * 128]
                qt_t = qtb[:, 0:2 * T]
                kt_t = ktb[:, 0:2 * T]
                prev = pair_done.get(pr - 2)
                prev1 = pair_done.get(pr - 1)

                def f_wqk(g, pr=pr, wv_=wv_, prev=prev):
                    if prev is not None:
                        g.wait_ge(p.pe.h, prev[0])
                    for hi, h in enumerate((2 * pr, 2 * pr + 1)):
                        for side, off in ((0, 0), (1, 1024)):
                            src = wqkv.rearrange("(dc q) m -> q dc m", q=128)[:, :, off + h * 64: off + (h + 1) * 64]
                            dst = wv_.rearrange("q (dc w) -> q dc w", dc=DC)[:, :, hi * 128 + side * 64: hi * 128 + (side + 1) * 64]
                            p.dg.inc(g.dma_start(out=dst, in_=src), 16)
                p.block.gpsimd(f_wqk)
                wn = p.dg.tok()

                ev_marks = {}
                kd_marks = {}
                for hi in range(2):
                    h = 2 * pr + hi
                    for qt in range(4):
                        i = hi * 4 + qt
                        sub = i % 2
                        pq = pa[:, sub * 512:(sub + 1) * 512]
                        emark = ev_marks.get(i - 2)

                        def f_qkm(t, wv_=wv_, hi=hi, qt=qt, wn=wn, pq=pq, emark=emark, prev1=prev1):
                            p.dg.wait(t, wn)
                            if emark is not None:
                                t.wait_ge(p.dve.h, emark)
                            elif prev1 is not None:
                                t.wait_ge(p.act.h, prev1[1])
                                t.wait_ge(p.dve.h, prev1[2])
                                p.dw.wait(t, prev1[3])
                            wr = wv_.rearrange("q (dc w) -> q dc w", dc=DC)
                            for dc in range(DC):
                                p.pe.inc(t.matmul(pq, wr[:, dc, hi * 128:(hi + 1) * 128],
                                         lnxfm[:, dc * T + qt * 512: dc * T + (qt + 1) * 512],
                                         start=(dc == 0), stop=(dc == DC - 1)))
                        p.block.tensor(f_qkm)
                        qn = p.pe.n
                        kmark = kd_marks.get(i - 2)

                        def f_qkev(v, pq=pq, qn=qn, hi=hi, qt=qt, h=h, qt_t=qt_t,
                                   i=i, kmark=kmark):
                            v.wait_ge(p.pe.h, qn)
                            if kmark is not None:
                                p.dw.wait(v, kmark)
                            ks = ktmp[64:128, (i % 2) * 512:(i % 2 + 1) * 512]
                            if qkb is not None:
                                v.tensor_scalar(qt_t[:, hi * T + qt * 512: hi * T + (qt + 1) * 512],
                                                pq[0:64, :], qkbs[0:64, h:h + 1], 0.125,
                                                ALU.add, ALU.mult)
                                p.dve.inc(v.tensor_scalar(ks, pq[64:128, :], qkbs[64:128, h:h + 1],
                                                          None, ALU.add))
                            else:
                                v.tensor_scalar_mul(qt_t[:, hi * T + qt * 512: hi * T + (qt + 1) * 512],
                                                    pq[0:64, :], 0.125)
                                p.dve.inc(v.tensor_copy(ks, pq[64:128, :]))
                        p.block.vector(f_qkev)
                        ev_marks[i] = p.dve.n
                        evn = p.dve.n

                        def f_kd(sy, hi=hi, qt=qt, i=i, evn=evn, kt_t=kt_t, prev1=prev1):
                            sy.wait_ge(p.dve.h, evn)
                            if i < 2 and prev1 is not None:
                                sy.wait_ge(p.pe.h, prev1[0])
                            p.dw.inc(sy.dma_start(
                                out=kt_t[:, hi * T + qt * 512: hi * T + (qt + 1) * 512],
                                in_=ktmp[64:128, (i % 2) * 512:(i % 2 + 1) * 512]), 16)
                        p.block.sync(f_kd)
                        kd_marks[i] = p.dw.tok()
                qkprojn = p.dve.n
                kdman = p.dw.tok()
                qkready = ("dve", qkprojn)

                if layer == 1:
                    rp_marks = {}
                    for idx4 in range(4):
                        ti, rhi = idx4 // 2, idx4 % 2
                        tile = (qt_t, kt_t)[ti]
                        tv = tile[0:64, rhi * T: rhi * T + T]
                        pav = pa[0:64, (idx4 % 2) * 2048:(idx4 % 2) * 2048 + 2048]
                        emark = rp_marks.get(idx4 - 2)

                        def f_rp(t, tv=tv, idx4=idx4, pav=pav, emark=emark,
                                 kdman=kdman):
                            if idx4 == 0:
                                t.wait_ge(p.dve.h, qkprojn)
                                p.dw.wait(t, kdman)
                            if emark is not None:
                                t.wait_ge(p.dve.h, emark)
                            for qt in range(4):
                                p.pe.inc(t.matmul(pav[:, qt * 512:(qt + 1) * 512],
                                         pmb[:, :],
                                         tv[:, qt * 512:(qt + 1) * 512],
                                         start=True, stop=True))
                        p.block.tensor(f_rp)
                        rpn = p.pe.n

                        def f_rv(v, tv=tv, pav=pav, rpn=rpn):
                            v.wait_ge(p.pe.h, rpn)
                            A = attnb[0:64, 0:2048]
                            v.tensor_mul(A, tv, cosb[:, :])
                            v.tensor_mul(tv, pav, sinb[:, :])
                            p.dve.inc(v.tensor_add(tv, tv, A))
                        p.block.vector(f_rv)
                        rp_marks[idx4] = p.dve.n
                    qkready = ("dve", p.dve.n)

                act_snap, dve_snap = p.act.n, p.dve.n

                def f_c0(v, qkready=qkready, act_snap=act_snap, kdman=kdman):
                    v.wait_ge(p.act.h, act_snap)
                    p.dw.wait(v, kdman)
                    p.dve.inc(v.memset(sc1[:, 0:2], 0.0))
                p.block.vector(f_c0)

                for side, tile in ((0, qtb), (1, ktb)):
                    tv = tile[:, 0:2 * T]
                    for half in range(2):
                        pe_now = p.pe.n

                        def f_sq(v, tv=tv, half=half, pe_now=pe_now):
                            v.wait_ge(p.pe.h, pe_now)
                            p.dve.inc(v.tensor_mul(sqb[:, :], tv[:, half * T:(half + 1) * T],
                                         tv[:, half * T:(half + 1) * T]))
                        p.block.vector(f_sq)
                        sqn = p.dve.n
                        dve_now = p.dve.n

                        def f_nmm(t, sqn=sqn, act_snap=act_snap, dve_now=dve_now):
                            t.wait_ge(p.dve.h, sqn)
                            t.wait_ge(p.act.h, act_snap)
                            t.wait_ge(p.dve.h, dve_now)
                            for qt in range(4):
                                p.pe.inc(t.matmul(pa[0:1, 2048 + qt * 512:2048 + (qt + 1) * 512],
                                         C["ones_bf"][0:64, 0:1],
                                         sqb[:, qt * 512:(qt + 1) * 512], start=True, stop=True))
                        p.block.tensor(f_nmm)
                        nmn = p.pe.n

                        def f_nred(v, side=side, nmn=nmn):
                            v.wait_ge(p.pe.h, nmn)
                            p.dve.inc(v.tensor_reduce(sc1[0:1, 4:5], pa[0:1, 2048:4096], AX.X, ALU.max))
                            v.wait_ge(p.dve.h, p.dve.n)
                            p.dve.inc(v.tensor_tensor(sc1[0:1, side:side + 1], sc1[0:1, side:side + 1],
                                            sc1[0:1, 4:5], ALU.max))
                        p.block.vector(f_nred)
                nrn = p.dve.n

                def f_cln(s, nrn=nrn):
                    s.wait_ge(p.dve.h, nrn)
                    p.act.inc(s.activation(sc1[0:1, 0:2], sc1[0:1, 0:2], AF.Ln))
                p.block.scalar(f_cln)
                cn = p.act.n

                def f_cc(v, cn=cn):
                    v.wait_ge(p.act.h, cn)
                    p.dve.inc(v.tensor_add(sc1[0:1, 5:6], sc1[0:1, 0:1], sc1[0:1, 1:2]))
                    v.wait_ge(p.dve.h, p.dve.n)
                    p.dve.inc(v.tensor_scalar(sc1[0:1, 5:6], sc1[0:1, 5:6], 0.5, 0.1, ALU.mult, ALU.add))
                p.block.vector(f_cc)
                ccn = p.dve.n

                def f_cexp(s, ccn=ccn):
                    s.wait_ge(p.dve.h, ccn)
                    p.act.inc(s.activation(sc1[0:1, 6:7], sc1[0:1, 5:6], AF.Exp))
                p.block.scalar(f_cexp)
                cen = p.act.n

                def f_cneg(v, cen=cen):
                    v.wait_ge(p.act.h, cen)
                    p.dve.inc(v.tensor_scalar_mul(sc1[0:1, 7:8], sc1[0:1, 6:7], -1.0))
                p.block.vector(f_cneg)
                cnn = p.dve.n

                def f_cbc(t, cnn=cnn):
                    t.wait_ge(p.dve.h, cnn)
                    p.pe.inc(t.matmul(pa[:, 2048:2049], C["ones_f32"][0:1, :], sc1[0:1, 7:8],
                             start=True, stop=True))
                p.block.tensor(f_cbc)
                cbn = p.pe.n

                def f_cbe(v, cbn=cbn):
                    v.wait_ge(p.pe.h, cbn)
                    p.dve.inc(v.tensor_copy(zcol[:, 4:5], pa[:, 2048:2049]))
                p.block.vector(f_cbe)
                cbev = p.dve.n
                negC = zcol[:, 4:5]

                if layer == 0:
                    for hi in range(2):
                        _l0_scores(p, C, vtm, attnb, vpb, zcol, pa, qt_t, kt_t,
                                   hi, 2 * pr + hi, cbev, negC)
                    ctxn = p.pe.n

                    def f_ctxev(s, pr=pr, ctxn=ctxn):
                        s.wait_ge(p.pe.h, ctxn)
                        p.act.inc(s.activation(ctxfm[:, pr * T:(pr + 1) * T], pa[:, 2048:4096], AF.Copy))
                    p.block.scalar(f_ctxev)
                    pair_done[pr] = (p.pe.n, p.act.n, p.dve.n, p.dw.tok())
                else:
                    cdm = None
                    for hi in range(2):
                        for qh in range(2):
                            cdm2 = _l1_headhalf(p, C, vtm, attnb, rzb, zbc, ctmp,
                                                ctxfm, pa, qt_t, kt_t, pr, hi,
                                                2 * pr + hi, qh, cbev, qkready, negC, cdm)
                            if cdm2 is not None:
                                cdm = cdm2
                    pair_done[pr] = (p.pe.n, p.act.n, p.dve.n, p.dw.tok())
        p.fence()

        with (
            nc.sbuf_tensor(None, [128, 8 * 1024], BF16, allow_name_mangling=True) as wob,
            nc.sbuf_tensor(None, [1, 1024], BF16, allow_name_mangling=True) as bob,
            nc.psum_tensor(None, [128, 2048], F32, allow_name_mangling=True) as po,
        ):
            def f_wo(g):
                src = wout.rearrange("(ch q) d -> q ch d", q=128)
                p.dg.inc(g.dma_start(out=wob.rearrange("q (ch d) -> q ch d", ch=8), in_=src), 16)
                p.dg.inc(g.dma_start(out=bob[:, :], in_=bout[:, :]), 16)
            p.block.gpsimd(f_wo)
            won = p.dg.tok()
            o_marks = {}
            for sc in range(NCH):
                par = sc % 2
                pov = po[:, par * 1024:(par + 1) * 1024]
                emark = o_marks.get(sc - 2)

                def f_omm(t, sc=sc, pov=pov, won=won, emark=emark):
                    p.dg.wait(t, won)
                    if emark is not None:
                        t.wait_ge(p.dve.h, emark)
                    for nh in range(2):
                        for ch in range(8):
                            t.matmul(pov[:, nh * 512:(nh + 1) * 512],
                                     ctxfm[:, ch * T + sc * 128: ch * T + (sc + 1) * 128],
                                     wob[:, ch * 1024 + nh * 512: ch * 1024 + (nh + 1) * 512],
                                     start=(ch == 0), stop=False)
                        p.pe.inc(t.matmul(pov[:, nh * 512:(nh + 1) * 512], C["ones_bf"][0:1, 0:128],
                                 bob[0:1, nh * 512:(nh + 1) * 512], start=False, stop=True))
                p.block.tensor(f_omm)
                on = p.pe.n
                xv = Xsb[:, sc * 1024:(sc + 1) * 1024]

                def f_oadd(v, pov=pov, on=on, xv=xv):
                    v.wait_ge(p.pe.h, on)
                    p.dve.inc(v.tensor_add(xv, xv, pov))
                p.block.vector(f_oadd)
                o_marks[sc] = p.dve.n
        p.fence()


def emit_gating(p, C, lnxfm, wgate, bgv, logit_sb):
    nc = p.nc
    with (
        nc.sbuf_tensor(None, [128, DC * 8], BF16, allow_name_mangling=True) as wgb,
        nc.sbuf_tensor(None, [1, 8], F32, allow_name_mangling=True) as bgs,
        nc.sbuf_tensor(None, [128, 8], F32, allow_name_mangling=True) as bgb,
        nc.psum_tensor(None, [128, 2048], F32, allow_name_mangling=True) as pg,
    ):
        def f_w(g):
            src = wgate.rearrange("(dc q) e -> q dc e", q=128)
            p.dg.inc(g.dma_start(out=wgb.rearrange("q (dc e) -> q dc e", dc=DC), in_=src), 16)
            p.dg.inc(g.dma_start(out=bgs[:, :], in_=bgv[:, :]), 16)
        p.block.gpsimd(f_w)
        wn = p.dg.tok()

        def f_bg(t):
            p.dg.wait(t, wn)
            p.pe.inc(t.matmul(pg[:, 0:8], C["ones_f32"][0:1, :], bgs[0:1, :], start=True, stop=True))
        p.block.tensor(f_bg)
        bn = p.pe.n

        def f_bge(v):
            v.wait_ge(p.pe.h, bn)
            p.dve.inc(v.tensor_copy(bgb[:, :], pg[:, 0:8]))
        p.block.vector(f_bge)
        bev = p.dve.n

        ev_marks = {}
        for tc in range(NCH):
            emark = ev_marks.get(tc - 4)
            slot = (tc % 4) * 512

            def f_mm(t, tc=tc, wn=wn, emark=emark, slot=slot):
                p.dg.wait(t, wn)
                if emark is not None:
                    t.wait_ge(p.dve.h, emark)
                else:
                    t.wait_ge(p.dve.h, bev)
                for dc in range(DC):
                    p.pe.inc(t.matmul(pg[:, slot:slot + 8],
                             lnxfm[:, dc * T + tc * 128: dc * T + (tc + 1) * 128],
                             wgb[:, dc * 8:(dc + 1) * 8],
                             start=(dc == 0), stop=(dc == DC - 1)))
            p.block.tensor(f_mm)
            mn = p.pe.n

            def f_ev(v, tc=tc, mn=mn, slot=slot):
                v.wait_ge(p.pe.h, mn)
                v.wait_ge(p.dve.h, bev)
                p.dve.inc(v.tensor_add(logit_sb[:, tc * 8:(tc + 1) * 8], pg[:, slot:slot + 8],
                             bgb[:, :]))
            p.block.vector(f_ev)
            ev_marks[tc] = p.dve.n
    p.fence()



def emit_routing(p, C, lg_d, stk):
    nc = p.nc
    es = stk.enter_context
    lg = es(nc.sbuf_tensor(None, [128, TCH * 8], F32, allow_name_mangling=True))
    lab = es(nc.sbuf_tensor(None, [128, TCH * 8], BF16, allow_name_mangling=True))
    scr = es(nc.sbuf_tensor(None, [128, TCH * 8], F32, allow_name_mangling=True))
    seqs = es(nc.sbuf_tensor(None, [128, 8 * 128], F32, allow_name_mangling=True))
    labT = es(nc.sbuf_tensor(None, [128, 8 * 128], F32, allow_name_mangling=True))
    mx = es(nc.sbuf_tensor(None, [128, 2 * TCH], F32, allow_name_mangling=True))
    offs = es(nc.sbuf_tensor(None, [128, 8], F32, allow_name_mangling=True))
    bt = es(nc.sbuf_tensor(None, [128, 8], F32, allow_name_mangling=True))
    row = es(nc.sbuf_tensor(None, [8, 2 * 128], F32, allow_name_mangling=True))

    def f_ld(g):
        p.dg.inc(g.dma_start(out=lg[:, :], in_=lg_d[:, :]), 16)
    p.block.gpsimd(f_ld)
    ln_ = p.dg.tok()

    lgv = lg[:, :].rearrange("q (c e) -> q c e", e=8)
    scrv = scr[:, :].rearrange("q (c e) -> q c e", e=8)
    labv = lab[:, :].rearrange("q (c e) -> q c e", e=8)
    m1 = mx[:, 0:TCH]
    m2 = mx[:, TCH:2 * TCH]

    def f_top2(v):
        p.dg.wait(v, ln_)
        p.dve.inc(v.tensor_reduce(m1, lgv, AX.X, ALU.max))
        v.wait_ge(p.dve.h, p.dve.n)
        m1b = m1.rearrange("q c -> q c ()").broadcast_to([128, TCH, 8])
        p.dve.inc(v.tensor_tensor(scrv, lgv, m1b, ALU.is_equal))
        v.wait_ge(p.dve.h, p.dve.n)
        p.dve.inc(v.scalar_tensor_tensor(scr[:, :], scr[:, :], -1e30, lg[:, :], ALU.mult, ALU.add))
        v.wait_ge(p.dve.h, p.dve.n)
        p.dve.inc(v.tensor_reduce(m2, scrv, AX.X, ALU.max))
        v.wait_ge(p.dve.h, p.dve.n)
        m2b = m2.rearrange("q c -> q c ()").broadcast_to([128, TCH, 8])
        p.dve.inc(v.tensor_tensor(scrv, scrv, m2b, ALU.is_equal))
        v.wait_ge(p.dve.h, p.dve.n)
        p.dve.inc(v.tensor_tensor(labv, lgv, m1b, ALU.is_equal))
        v.wait_ge(p.dve.h, p.dve.n)
        p.dve.inc(v.tensor_tensor(lab[:, :], lab[:, :], scr[:, :], ALU.add))
    p.block.vector(f_top2)
    topn = p.dve.n

    with (
        nc.psum_tensor(None, [128, 1024], BF16, allow_name_mangling=True) as ptb,
        nc.psum_tensor(None, [128, 512], F32, allow_name_mangling=True) as ptf,
    ):
        sc_marks = {}
        for e in range(8):
            def f_tr(t, e=e, topn=topn):
                if e == 0:
                    t.wait_ge(p.dve.h, topn)
                else:
                    t.wait_ge(p.dve.h, sc_marks[e - 1])
                src = lab[:, :].rearrange("q (c e) -> q c e", e=8)[:, :, e]
                p.pe.inc(t.transpose(ptb[:, 0:128], src, C["id128"][:, :]))
            p.block.tensor(f_tr)
            tn = p.pe.n

            def f_sc(v, e=e, tn=tn):
                v.wait_ge(p.pe.h, tn)
                p.dve.inc(v.tensor_copy(labT[:, e * 128:(e + 1) * 128], ptb[:, 0:128]))
                v.wait_ge(p.dve.h, p.dve.n)
                p.dve.inc(v.tensor_tensor_scan(seqs[:, e * 128:(e + 1) * 128],
                                     labT[:, e * 128:(e + 1) * 128],
                                     labT[:, e * 128:(e + 1) * 128], 0.0, ALU.add, ALU.bypass))
                v.wait_ge(p.dve.h, p.dve.n)
                p.dve.inc(v.tensor_copy(bt[:, e:e + 1], seqs[:, e * 128 + 127:e * 128 + 128]))
            p.block.vector(f_sc)
            sc_marks[e] = p.dve.n
        scn = p.dve.n

        def f_t2(t, scn=scn):
            t.wait_ge(p.dve.h, scn)
            p.pe.inc(t.transpose(ptf[:, 0:128].rearrange("a b -> a b") if False else ptf[0:8, 0:128],
                        bt[:, 0:8], C["id128f"][:, :]))
        p.block.tensor(f_t2)
        t2n = p.pe.n

        def f_off(v, t2n=t2n):
            v.wait_ge(p.pe.h, t2n)
            p.dve.inc(v.memset(row[0:8, 0:1], 0.0))
            p.dve.inc(v.tensor_copy(row[0:8, 1:128], ptf[0:8, 0:127]))
            v.wait_ge(p.dve.h, p.dve.n)
            p.dve.inc(v.tensor_tensor_scan(row[0:8, 128:256], row[0:8, 0:128], row[0:8, 0:128],
                                 0.0, ALU.add, ALU.bypass))
        p.block.vector(f_off)
        offn = p.dve.n

        def f_t3(t, offn=offn):
            t.wait_ge(p.dve.h, offn)
            p.pe.inc(t.transpose(ptf[:, 128:136], row[0:8, 128:256], C["id128f"][0:8, 0:8]))
        p.block.tensor(f_t3)
        t3n = p.pe.n

        def f_oe(v, t3n=t3n):
            v.wait_ge(p.pe.h, t3n)
            p.dve.inc(v.tensor_copy(offs[:, :], ptf[:, 128:136]))
            v.wait_ge(p.dve.h, p.dve.n)
            for e in range(8):
                p.dve.inc(v.tensor_scalar(seqs[:, e * 128:(e + 1) * 128], seqs[:, e * 128:(e + 1) * 128],
                                offs[:, e:e + 1], None, ALU.add))
        p.block.vector(f_oe)
    p.fence()
    return {"lg": lg, "lab": lab, "scr": scr, "seqs": seqs, "labT": labT, "mx": mx}



def build_l1(stages="fafg"):
    nc = bass.Bass("TRN2", target_bir_lowering=False)
    C = make_consts(nc)
    x_d = nc.dram_tensor("x", [T, D], F32, kind="ExternalInput").ap()
    wg_d = nc.dram_tensor("wg_t", [D, H0], I32, kind="ExternalInput").ap()
    wu_d = nc.dram_tensor("wu_t", [D, H0], I32, kind="ExternalInput").ap()
    wd_d = nc.dram_tensor("wd_t", [H0, D], I32, kind="ExternalInput").ap()
    wqkv0_d = nc.dram_tensor("wqkv0", [D, 3 * D], F32, kind="ExternalInput").ap()
    qkb_d = nc.dram_tensor("qkb0", [128, 16], F32, kind="ExternalInput").ap()
    vb_d = nc.dram_tensor("vb0", [1, D], F32, kind="ExternalInput").ap()
    lw_d = nc.dram_tensor("lw_t", [D, D], F32, kind="ExternalInput").ap()
    lb_d = nc.dram_tensor("lb0", [1, D], F32, kind="ExternalInput").ap()
    wgate_d = nc.dram_tensor("wgate", [D, 8], F32, kind="ExternalInput").ap()
    bg_d = nc.dram_tensor("bg", [1, 8], F32, kind="ExternalInput").ap()
    xo_d = nc.dram_tensor("xo", [T, D], F32, kind="ExternalOutput").ap()
    lnx_d = nc.dram_tensor("lnx", [T, D], BF16, kind="ExternalOutput").ap()
    lgo_d = nc.dram_tensor("lg", [T, 8], F32, kind="ExternalOutput").ap()

    with ExitStack() as stack:
        Xsb = stack.enter_context(nc.sbuf_tensor("Xsb", [128, NCH * 1024], F32))
        block = stack.enter_context(nc.Block())
        p = P(nc, block, stack)
        load_consts(p, C, ["id128", "ones_bf", "ones_f32", "qkscale"])

        def f_ldx(sy):
            p.dw.inc(sy.dma_start(out=Xsb[:, :].rearrange("q (c d) -> q c d", c=NCH),
                                  in_=x_d.rearrange("(c q) d -> q c d", q=128)), 16)
        p.block.sync(f_ldx)
        p.fence()

        if "f" in stages:
            emit_ffn0(p, C, Xsb[:, :], wg_d, wu_d, wd_d)
        if "a" in stages:
            emit_attn(p, C, Xsb[:, :], 0, wqkv0_d, qkb=qkb_d, vb=vb_d, wout=lw_d, bout=lb_d)
        if stages.count("f") > 1:
            emit_ffn0(p, C, Xsb[:, :], wg_d, wu_d, wd_d)

        with (
            nc.sbuf_tensor("lnxf4", [128, DC * T], BF16) as lnxfm,
            nc.sbuf_tensor("lnxt4", [128, NCH * 1024], BF16) as lnxtm,
            nc.sbuf_tensor("lgsb", [128, NCH * 8], F32) as lgsb,
        ):
            emit_ln(p, C, Xsb[:, :], lnxfm[:, :], 0, NCH, lnxtm=lnxtm[:, :])
            if "g" in stages:
                emit_gating(p, C, lnxfm[:, :], wgate_d, bg_d, lgsb[:, :])

            def f_st(sy):
                p.dw.inc(sy.dma_start(out=xo_d.rearrange("(c q) d -> q c d", q=128),
                                      in_=Xsb[:, :].rearrange("q (c d) -> q c d", c=NCH)), 16)
                p.dw.inc(sy.dma_start(out=lnx_d.rearrange("(c q) d -> q c d", q=128),
                                      in_=lnxtm[:, :].rearrange("q (c d) -> q c d", c=NCH)), 16)
                p.dw.inc(sy.dma_start(out=lgo_d.rearrange("(c q) e -> q c e", q=128),
                                      in_=lgsb[:, :].rearrange("q (c e) -> q c e", c=NCH)), 16)
                p.dw.wait(sy, p.dw.tok())
            p.block.sync(f_st)
    return nc


def build_l2a():
    nc = bass.Bass("TRN2", target_bir_lowering=False)
    C = make_consts(nc)
    lg_d = nc.dram_tensor("lga", [128, TCH * 8], F32, kind="ExternalInput").ap()
    lnxa_d = nc.dram_tensor("lnxa", [TALL + 1, D], BF16, kind="ExternalInput").ap()
    w1_d = nc.dram_tensor("w1", [D, HM], F32, kind="ExternalInput").ap()
    w2_d = nc.dram_tensor("w2", [D, HM], F32, kind="ExternalInput").ap()
    w3_d = nc.dram_tensor("w3", [HM, D], F32, kind="ExternalInput").ap()
    esel_d = nc.dram_tensor("esel", [128, 8], F32, kind="ExternalInput").ap()
    o_d = nc.dram_tensor("o", [CAP, D], F32, kind="ExternalOutput").ap()
    g_dram = nc.dram_tensor("gbuf", [CAP + 1, 2], I32, kind="Internal").ap()
    h_dram = nc.dram_tensor("hbuf", [HM // 128, 128, CAP], BF16, kind="Internal").ap()
    ginit = nc.inline_tensor(np.full((CAP + 1, 2), SENT, np.int32), "c_ginit").ap()

    with ExitStack() as stack:
        block = stack.enter_context(nc.Block())
        p = P(nc, block, stack)
        load_consts(p, C, ["id128", "id128f", "tid2"])
        rstk = stack.enter_context(ExitStack())
        r = emit_routing(p, C, lg_d, rstk)
        seqs, labT = r["seqs"], r["labT"]

        with (
            nc.sbuf_tensor("ids", [128, 4 * 128], F32) as ids,
            nc.sbuf_tensor("idsi", [128, 128], I32) as idsi,
            nc.sbuf_tensor("esel_sb", [128, 8], F32) as esel,
        ):
            def f_le(g):
                p.dg.inc(g.dma_start(out=esel[:, :], in_=esel_d[:, :]), 16)
                p.dg.inc(g.dma_start(out=g_dram[:, :], in_=ginit[:, :]), 16)
            p.block.gpsimd(f_le)
            len_ = p.dg.tok()

            def f_idx(v):
                p.dg.wait(v, len_)
                sq = ids[:, 0:128]
                lb = ids[:, 128:256]
                vm = ids[:, 256:384]
                ix = ids[:, 384:512]
                p.dve.inc(v.memset(sq, 0.0))
                p.dve.inc(v.memset(lb, 0.0))
                v.wait_ge(p.dve.h, p.dve.n)
                for e in range(8):
                    p.dve.inc(v.scalar_tensor_tensor(sq, seqs[:, e * 128:(e + 1) * 128],
                                           esel[:, e:e + 1], sq, ALU.mult, ALU.add))
                    p.dve.inc(v.scalar_tensor_tensor(lb, labT[:, e * 128:(e + 1) * 128],
                                           esel[:, e:e + 1], lb, ALU.mult, ALU.add))
                    v.wait_ge(p.dve.h, p.dve.n)
                p.dve.inc(v.tensor_single_scalar(vm, sq, float(CAP), ALU.is_le))
                v.wait_ge(p.dve.h, p.dve.n)
                p.dve.inc(v.tensor_mul(vm, vm, lb))
                v.wait_ge(p.dve.h, p.dve.n)
                p.dve.inc(v.tensor_scalar_add(ix, sq, -float(CAP + 1)))
                v.wait_ge(p.dve.h, p.dve.n)
                p.dve.inc(v.tensor_mul(ix, ix, vm))
                v.wait_ge(p.dve.h, p.dve.n)
                p.dve.inc(v.tensor_scalar_add(ix, ix, float(CAP)))
                v.wait_ge(p.dve.h, p.dve.n)
                p.dve.inc(v.tensor_copy(idsi[:, :], ix))
            p.block.vector(f_idx)
            idn = p.dve.n

            def f_scat(g, idn=idn):
                g.wait_ge(p.dve.h, idn)
                for f in range(128):
                    p.dg.inc(g.indirect_dma_start(
                        g_dram[0:CAP + 1, :], IndirectOffsetOnAxis(idsi[:, f:f + 1], 0),
                        C["tid2"][:, 2 * f:2 * f + 2], None), 16)
            p.block.gpsimd(f_scat)
        p.fence()
        rstk.close()

        with (
            nc.sbuf_tensor("gsb", [128, 2 * NSB], I32) as gsb,
            nc.sbuf_tensor("xett", [128, 4 * 1024], BF16) as xett,
            nc.sbuf_tensor("xe", [128, DC * CAP], BF16) as xe,
        ):
            ptx_cm = nc.psum_tensor("ptx", [128, 2048], BF16)
            ptx = ptx_cm.__enter__()

            def f_gl(g):
                src = g_dram[0:CAP, :].rearrange("(b q) w -> q b w", q=128)
                p.dg.inc(g.dma_start(out=gsb[:, :].rearrange("q (b w) -> q b w", w=2), in_=src), 16)
            p.block.gpsimd(f_gl)
            gln = p.dg.tok()

            tr_marks, te_marks = {}, {}
            for b in range(NSB):
                def f_ga(g, b=b, gln=gln):
                    p.dg.wait(g, gln)
                    if b >= 4:
                        g.wait_ge(p.pe.h, tr_marks[b - 4])
                    p.dg.inc(g.indirect_dma_start(
                        xett[:, (b % 4) * 1024:(b % 4 + 1) * 1024], None,
                        lnxa_d[0:TALL + 1, :],
                        IndirectOffsetOnAxis(gsb[:, 2 * b:2 * b + 1], 0)), 16)
                p.block.gpsimd(f_ga)
                gn = p.dg.tok()

                def f_tr(t, b=b, gn=gn):
                    p.dg.wait(t, gn)
                    if b >= 2:
                        t.wait_ge(p.dve.h, te_marks[b - 2])
                    for dc in range(DC):
                        p.pe.inc(t.transpose(ptx[:, (b % 2) * 1024 + dc * 128:(b % 2) * 1024 + (dc + 1) * 128],
                                    xett[:, (b % 4) * 1024 + dc * 128:(b % 4) * 1024 + (dc + 1) * 128],
                                    C["id128"][:, :]))
                p.block.tensor(f_tr)
                tr_marks[b] = p.pe.n
                tn = p.pe.n

                def f_te(v, b=b, tn=tn):
                    v.wait_ge(p.pe.h, tn)
                    for dc in range(DC):
                        p.dve.inc(v.tensor_copy(xe[:, dc * CAP + b * 128: dc * CAP + (b + 1) * 128],
                                      ptx[:, (b % 2) * 1024 + dc * 128:(b % 2) * 1024 + (dc + 1) * 128]))
                p.block.vector(f_te)
                te_marks[b] = p.dve.n
            p.fence()
            ptx_cm.__exit__(None, None, None)

            HGA = 512
            with (
                nc.sbuf_tensor("wab", [128, 4 * DC * HGA], BF16) as wab,
                nc.sbuf_tensor("w3sb", [128, 2 * 4 * 1024], BF16) as w3sb,
                nc.sbuf_tensor("sga", [128, 2 * 512], F32) as sga,
                nc.sbuf_tensor("hb", [128, 4 * 1536], BF16) as hb,
                nc.sbuf_tensor("osb", [128, 12 * 1024], F32) as osb,
                nc.psum_tensor("psa", [128, 2048], F32) as psa,
                nc.psum_tensor("psd", [128, 2048], F32) as psd,
            ):
                pe_marks = []
                dve_marks = {}
                acc_marks = []
                i = 0
                ost_tok = None
                for ho, TH in ((0, 1536), (1536, 1024)):
                    TB = TH // 128
                    SG = TH // 512
                    for hg in range(HM // HGA):
                        par = hg % 2
                        w1v = wab[:, (2 * par) * DC * HGA:(2 * par + 1) * DC * HGA]
                        w2v = wab[:, (2 * par + 1) * DC * HGA:(2 * par + 2) * DC * HGA]
                        w3v = w3sb[:, par * 4 * 1024:(par + 1) * 4 * 1024]
                        prev = pe_marks[-2] if len(pe_marks) >= 2 else None

                        def f_w(g, hg=hg, w1v=w1v, w2v=w2v, w3v=w3v, prev=prev):
                            if prev is not None:
                                g.wait_ge(p.pe.h, prev)
                            for w_d_, dst in ((w1_d, w1v), (w2_d, w2v)):
                                src = w_d_.rearrange("(dc q) h -> q dc h", q=128)[:, :, hg * HGA:(hg + 1) * HGA]
                                p.dg.inc(g.dma_start(out=dst.rearrange("q (dc h) -> q dc h", dc=DC),
                                                     in_=src), 16)
                            src3 = w3_d.rearrange("(hc q) d -> q hc d", q=128)[:, hg * 4:(hg + 1) * 4, :]
                            p.dg.inc(g.dma_start(out=w3v.rearrange("q (hc d) -> q hc d", hc=4),
                                                 in_=src3), 16)
                        p.block.gpsimd(f_w)
                        wn = p.dg.tok()

                        for hc in range(HGA // 128):
                            for sg in range(SG):
                                sub = i % 2
                                pg = psa[:, sub * 1024:sub * 1024 + 512]
                                pu = psa[:, sub * 1024 + 512:sub * 1024 + 1024]
                                emark = dve_marks.get(i - 2)

                                def f_mm(t, w1v=w1v, w2v=w2v, hc=hc, sg=sg, wn=wn,
                                         ho=ho, pg=pg, pu=pu, emark=emark):
                                    p.dg.wait(t, wn)
                                    if emark is not None:
                                        t.wait_ge(p.dve.h, emark)
                                    for wv_, pp in ((w1v, pg), (w2v, pu)):
                                        wr = wv_.rearrange("q (dc h) -> q dc h", dc=DC)
                                        for dc in range(DC):
                                            p.pe.inc(t.matmul(pp, wr[:, dc, hc * 128:(hc + 1) * 128],
                                                     xe[:, dc * CAP + ho + sg * 512: dc * CAP + ho + (sg + 1) * 512],
                                                     start=(dc == 0), stop=(dc == DC - 1)))
                                p.block.tensor(f_mm)
                                mmn = p.pe.n
                                sgv = sga[:, (i % 2) * 512:(i % 2 + 1) * 512]

                                def f_silu(s, pg=pg, mmn=mmn, sgv=sgv, emark=emark):
                                    s.wait_ge(p.pe.h, mmn)
                                    if emark is not None:
                                        s.wait_ge(p.dve.h, emark)
                                    p.act.inc(s.activation(sgv, pg, AF.Silu))
                                p.block.scalar(f_silu)
                                sn = p.act.n
                                hv = hb[:, hc * TH + sg * 512: hc * TH + (sg + 1) * 512]

                                def f_h(v, pu=pu, sgv=sgv, hv=hv, sn=sn):
                                    v.wait_ge(p.act.h, sn)
                                    p.dve.inc(v.scalar_tensor_tensor(hv, pu, 1.0, sgv, ALU.mult, ALU.mult))
                                p.block.vector(f_h)
                                dve_marks[i] = p.dve.n
                                i += 1
                        hrdy = p.dve.n

                        for tb in range(TB):
                            pdv = psd[:, (tb % 2) * 1024:(tb % 2 + 1) * 1024]
                            amark = acc_marks[-2] if len(acc_marks) >= 2 else None

                            def f_dn(t, tb=tb, hg=hg, TH=TH, pdv=pdv, hrdy=hrdy,
                                     w3v=w3v, wn=wn, amark=amark):
                                if tb == 0:
                                    t.wait_ge(p.dve.h, hrdy)
                                    p.dg.wait(t, wn)
                                if amark is not None:
                                    t.wait_ge(p.dve.h, amark)
                                for hc in range(4):
                                    for nh in range(2):
                                        p.pe.inc(t.matmul(pdv[:, nh * 512:(nh + 1) * 512],
                                                 hb[:, hc * TH + tb * 128: hc * TH + (tb + 1) * 128],
                                                 w3v[:, hc * 1024 + nh * 512: hc * 1024 + (nh + 1) * 512],
                                                 start=(hc == 0), stop=(hc == 3)))
                            p.block.tensor(f_dn)
                            dnn = p.pe.n
                            ov = osb[:, tb * 1024:(tb + 1) * 1024]

                            def f_acc(v, pdv=pdv, dnn=dnn, ov=ov, hg=hg,
                                      ost_tok=ost_tok):
                                v.wait_ge(p.pe.h, dnn)
                                if hg == 0:
                                    if ost_tok is not None:
                                        p.dw.wait(v, ost_tok)
                                    p.dve.inc(v.tensor_copy(ov, pdv))
                                else:
                                    p.dve.inc(v.tensor_add(ov, ov, pdv))
                            p.block.vector(f_acc)
                            acc_marks.append(p.dve.n)

                            if hg == HM // HGA - 1:
                                oen = p.dve.n

                                def f_os(sy, ho=ho, tb=tb, ov=ov, oen=oen):
                                    sy.wait_ge(p.dve.h, oen)
                                    p.dw.inc(sy.dma_start(
                                        out=o_d[ho + tb * 128: ho + (tb + 1) * 128, :],
                                        in_=ov), 16)
                                p.block.sync(f_os)
                        pe_marks.append(p.pe.n)
                    ost_tok = p.dw.tok()

                def f_fin(sy):
                    p.dw.wait(sy, p.dw.tok())
                p.block.sync(f_fin)
    return nc


def emit_combine(p, C, Xsb, lg_d, lgown_d, oall_d, ownsel_d):
    nc = p.nc
    rstk = ExitStack()
    r = emit_routing(p, C, lg_d, rstk)
    seqs = r["seqs"]

    with (
        nc.sbuf_tensor(None, [128, 8 * 128], F32, allow_name_mangling=True) as stm,
        nc.sbuf_tensor(None, [128, 8], F32, allow_name_mangling=True) as ownsel,
        nc.sbuf_tensor(None, [128, NCH * 8], F32, allow_name_mangling=True) as sqo,
        nc.sbuf_tensor(None, [128, NCH * 8], F32, allow_name_mangling=True) as lgo,
        nc.sbuf_tensor(None, [128, NCH * 8], F32, allow_name_mangling=True) as wrk,
        nc.sbuf_tensor(None, [128, NCH * 8], F32, allow_name_mangling=True) as wrk2,
        nc.sbuf_tensor(None, [128, 16 * NCH], F32, allow_name_mangling=True) as tok,
        nc.sbuf_tensor(None, [128, 2 * NCH], I32, allow_name_mangling=True) as gidx,
        nc.sbuf_tensor(None, [128, 2 * 1024], F32, allow_name_mangling=True) as ob,
        nc.psum_tensor(None, [128, 1024], F32, allow_name_mangling=True) as ptf,
    ):
        def f_ld(g):
            p.dg.inc(g.dma_start(out=ownsel[:, :], in_=ownsel_d[:, :]), 16)
            p.dg.inc(g.dma_start(out=lgo[:, :], in_=lgown_d[:, :]), 16)
        p.block.gpsimd(f_ld)
        ldn = p.dg.tok()

        tm_marks = {}
        for e in range(8):
            def f_tr(t, e=e):
                if e >= 2:
                    t.wait_ge(p.dve.h, tm_marks[e - 2])
                p.pe.inc(t.transpose(ptf[:, (e % 2) * 512:(e % 2) * 512 + 128],
                            seqs[:, e * 128:(e + 1) * 128], C["id128f"][:, :]))
            p.block.tensor(f_tr)
            tn = p.pe.n

            def f_ev(v, e=e, tn=tn):
                v.wait_ge(p.pe.h, tn)
                dst = stm[:, :].rearrange("q (c e) -> q c e", e=8)[:, :, e]
                p.dve.inc(v.tensor_copy(dst, ptf[:, (e % 2) * 512:(e % 2) * 512 + 128]))
            p.block.vector(f_ev)
            tm_marks[e] = p.dve.n
        smn = p.dve.n

        def f_own0(v, smn=smn, ldn=ldn):
            p.dg.wait(v, ldn)
            v.wait_ge(p.dve.h, smn)
            p.dve.inc(v.memset(sqo[:, :], 0.0))
            for g in range(8):
                v.wait_ge(p.dve.h, p.dve.n)
                p.dve.inc(v.scalar_tensor_tensor(sqo[:, :], stm[:, g * NCH * 8:(g + 1) * NCH * 8],
                                       ownsel[:, g:g + 1], sqo[:, :], ALU.mult, ALU.add))
        p.block.vector(f_own0)

        lgv = lgo[:, :].rearrange("q (c e) -> q c e", e=8)

        def f_own(v):
            m1 = tok[:, 0:NCH]
            m2 = tok[:, NCH:2 * NCH]
            i1 = tok[:, 2 * NCH:3 * NCH]
            i2 = tok[:, 3 * NCH:4 * NCH]
            s1 = tok[:, 4 * NCH:5 * NCH]
            s2 = tok[:, 5 * NCH:6 * NCH]
            dv = tok[:, 8 * NCH:9 * NCH]
            def q(inst):
                p.dve.inc(inst)
                v.wait_ge(p.dve.h, p.dve.n)
            q(v.tensor_reduce(m1, lgv, AX.X, ALU.max))
            m1b = m1.rearrange("q c -> q c ()").broadcast_to([128, NCH, 8])
            q(v.tensor_tensor(wrk.rearrange("q (c e) -> q c e", e=8), lgv, m1b, ALU.is_equal))
            q(v.scalar_tensor_tensor(wrk2[:, :], wrk[:, :], -1e30, lgo[:, :], ALU.mult, ALU.add))
            q(v.tensor_reduce(m2, wrk2.rearrange("q (c e) -> q c e", e=8), AX.X, ALU.max))
            m2b = m2.rearrange("q c -> q c ()").broadcast_to([128, NCH, 8])
            q(v.tensor_tensor(wrk2.rearrange("q (c e) -> q c e", e=8),
                            wrk2.rearrange("q (c e) -> q c e", e=8), m2b, ALU.is_equal))
            q(v.tensor_mul(wrk[:, :], wrk[:, :], C["iota_e"][:, :]))
            q(v.tensor_reduce(i1, wrk.rearrange("q (c e) -> q c e", e=8), AX.X, ALU.add))
            q(v.tensor_tensor(wrk.rearrange("q (c e) -> q c e", e=8), lgv, m1b, ALU.is_equal))
            q(v.tensor_mul(wrk[:, :], wrk[:, :], sqo[:, :]))
            q(v.tensor_reduce(s1, wrk.rearrange("q (c e) -> q c e", e=8), AX.X, ALU.add))
            q(v.tensor_mul(wrk2[:, :], wrk2[:, :], C["iota_e"][:, :]))
            q(v.tensor_reduce(i2, wrk2.rearrange("q (c e) -> q c e", e=8), AX.X, ALU.add))
            q(v.tensor_tensor(wrk.rearrange("q (c e) -> q c e", e=8), lgv, m1b, ALU.is_equal))
            q(v.scalar_tensor_tensor(wrk2[:, :], wrk[:, :], -1e30, lgo[:, :], ALU.mult, ALU.add))
            q(v.tensor_tensor(wrk2.rearrange("q (c e) -> q c e", e=8),
                            wrk2.rearrange("q (c e) -> q c e", e=8), m2b, ALU.is_equal))
            q(v.tensor_mul(wrk2[:, :], wrk2[:, :], sqo[:, :]))
            q(v.tensor_reduce(s2, wrk2.rearrange("q (c e) -> q c e", e=8), AX.X, ALU.add))
            p.dve.inc(v.tensor_sub(dv, m2, m1))
        p.block.vector(f_own)
        on1 = p.dve.n

        def f_sig(s, on1=on1):
            s.wait_ge(p.dve.h, on1)
            p.act.inc(s.activation(tok[:, 9 * NCH:10 * NCH], tok[:, 8 * NCH:9 * NCH], AF.Exp))
        p.block.scalar(f_sig)
        sgn = p.act.n

        def f_own2(v, sgn=sgn):
            v.wait_ge(p.act.h, sgn)
            i1 = tok[:, 2 * NCH:3 * NCH]; i2 = tok[:, 3 * NCH:4 * NCH]
            s1 = tok[:, 4 * NCH:5 * NCH]; s2 = tok[:, 5 * NCH:6 * NCH]
            w1 = tok[:, 6 * NCH:7 * NCH]; w2 = tok[:, 7 * NCH:8 * NCH]
            ex = tok[:, 9 * NCH:10 * NCH]
            g1 = tok[:, 10 * NCH:11 * NCH]; g2 = tok[:, 11 * NCH:12 * NCH]
            vv1 = tok[:, 12 * NCH:13 * NCH]; vv2 = tok[:, 13 * NCH:14 * NCH]
            def q(inst):
                p.dve.inc(inst)
                v.wait_ge(p.dve.h, p.dve.n)
            q(v.tensor_scalar_add(w1, ex, 1.0))
            q(v.reciprocal(w1, w1))
            q(v.tensor_mul(w2, ex, w1))
            q(v.tensor_single_scalar(vv1, s1, float(CAP), ALU.is_le))
            q(v.tensor_single_scalar(vv2, s2, float(CAP), ALU.is_le))
            q(v.tensor_mul(w1, w1, vv1))
            q(v.tensor_mul(w2, w2, vv2))
            q(v.tensor_scalar_mul(w1, w1, 0.5))
            q(v.tensor_scalar_mul(w2, w2, 0.5))
            for gg, ii, ss, vv in ((g1, i1, s1, vv1), (g2, i2, s2, vv2)):
                q(v.tensor_scalar(gg, ii, float(CAP), -1.0, ALU.mult, ALU.add))
                q(v.tensor_add(gg, gg, ss))
                q(v.tensor_scalar_add(gg, gg, -float(ODUMP)))
                q(v.tensor_mul(gg, gg, vv))
                q(v.tensor_scalar_add(gg, gg, float(ODUMP)))
            v.tensor_copy(gidx[:, 0:NCH], tok[:, 10 * NCH:11 * NCH])
            p.dve.inc(v.tensor_copy(gidx[:, NCH:2 * NCH], tok[:, 11 * NCH:12 * NCH]))
        p.block.vector(f_own2)
        on2 = p.dve.n

        add_marks = {}
        for b in range(NCH):
            amark = add_marks.get(b - 1)

            def f_g2(g, b=b, on2=on2, amark=amark):
                g.wait_ge(p.dve.h, on2)
                if amark is not None:
                    g.wait_ge(p.dve.h, amark)
                p.dg.inc(g.indirect_dma_start(
                    ob[:, 0:1024], None, oall_d[0:NCORES * CAP + 1, :],
                    IndirectOffsetOnAxis(gidx[:, b:b + 1], 0)), 16)
                p.dg.inc(g.indirect_dma_start(
                    ob[:, 1024:2048], None, oall_d[0:NCORES * CAP + 1, :],
                    IndirectOffsetOnAxis(gidx[:, NCH + b:NCH + b + 1], 0)), 16)
            p.block.gpsimd(f_g2)
            g2n = p.dg.tok()
            xv = Xsb[:, b * 1024:(b + 1) * 1024]

            def f_add(v, b=b, g2n=g2n, xv=xv):
                p.dg.wait(v, g2n)
                v.scalar_tensor_tensor(xv, ob[:, 0:1024], tok[:, 6 * NCH + b:6 * NCH + b + 1],
                                       xv, ALU.mult, ALU.add)
                p.dve.inc(v.scalar_tensor_tensor(xv, ob[:, 1024:2048], tok[:, 7 * NCH + b:7 * NCH + b + 1],
                                       xv, ALU.mult, ALU.add))
            p.block.vector(f_add)
            add_marks[b] = p.dve.n
    p.fence()
    rstk.close()


def build_l2b(final=False):
    nc = bass.Bass("TRN2", target_bir_lowering=False)
    C = make_consts(nc)
    x_d = nc.dram_tensor("x", [T, D], F32, kind="ExternalInput").ap()
    lg_d = nc.dram_tensor("lga", [128, TCH * 8], F32, kind="ExternalInput").ap()
    lgown_d = nc.dram_tensor("lgown", [128, NCH * 8], F32, kind="ExternalInput").ap()
    oall_d = nc.dram_tensor("oall", [NCORES * CAP + 1, D], F32, kind="ExternalInput").ap()
    ownsel_d = nc.dram_tensor("ownsel", [128, 8], F32, kind="ExternalInput").ap()
    if not final:
        wqkv_d = nc.dram_tensor("wqkv1", [D, 3 * D], F32, kind="ExternalInput").ap()
        wout_d = nc.dram_tensor("wout1", [D, D], F32, kind="ExternalInput").ap()
        bout_d = nc.dram_tensor("bout1", [1, D], F32, kind="ExternalInput").ap()
        wgate_d = nc.dram_tensor("wgate", [D, 8], F32, kind="ExternalInput").ap()
        bg_d = nc.dram_tensor("bg", [1, 8], F32, kind="ExternalInput").ap()
        xo_d = nc.dram_tensor("xo", [T, D], F32, kind="ExternalOutput").ap()
        lnx_d = nc.dram_tensor("lnx", [T, D], BF16, kind="ExternalOutput").ap()
        lgo_d = nc.dram_tensor("lg", [T, 8], F32, kind="ExternalOutput").ap()
    else:
        y_d = nc.dram_tensor("y", [T, D], F32, kind="ExternalOutput").ap()

    with ExitStack() as stack:
        Xsb = stack.enter_context(nc.sbuf_tensor("Xsb", [128, NCH * 1024], F32))
        block = stack.enter_context(nc.Block())
        p = P(nc, block, stack)
        keys = ["id128", "id128f", "iota_e"]
        if not final:
            keys += ["ones_bf", "ones_f32", "qkscale"]
        load_consts(p, C, keys)

        def f_ldx(sy):
            p.dw.inc(sy.dma_start(out=Xsb[:, :].rearrange("q (c d) -> q c d", c=NCH),
                                  in_=x_d.rearrange("(c q) d -> q c d", q=128)), 16)
        p.block.sync(f_ldx)
        p.fence()

        emit_combine(p, C, Xsb[:, :], lg_d, lgown_d, oall_d, ownsel_d)

        if final:
            def f_sty(sy):
                sy.wait_ge(p.dve.h, p.dve.n)
                p.dw.inc(sy.dma_start(out=y_d.rearrange("(c q) d -> q c d", q=128),
                                      in_=Xsb[:, :].rearrange("q (c d) -> q c d", c=NCH)), 16)
                p.dw.wait(sy, p.dw.tok())
            p.block.sync(f_sty)
        else:
            emit_attn(p, C, Xsb[:, :], 1, wqkv_d, wout=wout_d, bout=bout_d)
            with (
                nc.sbuf_tensor("lnxf6", [128, DC * T], BF16) as lnxfm,
                nc.sbuf_tensor("lnxt6", [128, NCH * 1024], BF16) as lnxtm,
                nc.sbuf_tensor("lgsb", [128, NCH * 8], F32) as lgsb,
            ):
                emit_ln(p, C, Xsb[:, :], lnxfm[:, :], 0, NCH, lnxtm=lnxtm[:, :])
                emit_gating(p, C, lnxfm[:, :], wgate_d, bg_d, lgsb[:, :])

                def f_st(sy):
                    p.dw.inc(sy.dma_start(out=xo_d.rearrange("(c q) d -> q c d", q=128),
                                          in_=Xsb[:, :].rearrange("q (c d) -> q c d", c=NCH)), 16)
                    p.dw.inc(sy.dma_start(out=lnx_d.rearrange("(c q) d -> q c d", q=128),
                                          in_=lnxtm[:, :].rearrange("q (c d) -> q c d", c=NCH)), 16)
                    p.dw.inc(sy.dma_start(out=lgo_d.rearrange("(c q) e -> q c e", q=128),
                                          in_=lgsb[:, :].rearrange("q (c e) -> q c e", c=NCH)), 16)
                    p.dw.wait(sy, p.dw.tok())
                p.block.sync(f_st)
    return nc



_PROGS = {}


def _progs():
    if not _PROGS:
        _PROGS["l1"] = build_l1()
        _PROGS["l2a"] = build_l2a()
        _PROGS["l2b"] = build_l2b(final=False)
        _PROGS["l3b"] = build_l2b(final=True)
    return _PROGS


def kernel(L0_attn_x, L0_attn_q_w, L0_attn_q_b, L0_attn_k_w, L0_attn_k_b,
           L0_attn_v_w, L0_attn_v_b, L0_attn_l_w, L0_attn_l_b,
           L0_ffn_w_gate, L0_ffn_w_up, L0_ffn_w_down,
           L1_attn_w_qkv, L1_attn_w_out, L1_attn_b_out,
           L1_ffn_Wg, L1_ffn_bg, L1_ffn_W1, L1_ffn_W2, L1_ffn_W3):
    pr = _progs()
    f32 = np.float32
    cc = np.ascontiguousarray

    x = np.asarray(L0_attn_x, f32)
    wg_t = cc(np.asarray(L0_ffn_w_gate).T)
    wu_t = cc(np.asarray(L0_ffn_w_up).T)
    wd_t = cc(np.asarray(L0_ffn_w_down).T)
    qT = np.asarray(L0_attn_q_w, f32).transpose(2, 0, 1).reshape(D, D)
    kT = np.asarray(L0_attn_k_w, f32).transpose(2, 0, 1).reshape(D, D)
    vT = np.asarray(L0_attn_v_w, f32).transpose(2, 0, 1).reshape(D, D)
    wqkv0 = cc(np.concatenate([qT, kT, vT], axis=1))
    qkb0 = np.zeros((128, 16), f32)
    qkb0[:64, :] = np.asarray(L0_attn_q_b, f32).reshape(16, 64).T
    qkb0[64:, :] = np.asarray(L0_attn_k_b, f32).reshape(16, 64).T
    vb0 = np.asarray(L0_attn_v_b, f32).reshape(1, D)
    lw_t = cc(np.asarray(L0_attn_l_w, f32).T)
    lb0 = np.asarray(L0_attn_l_b, f32).reshape(1, D)
    wqkv1 = cc(np.asarray(L1_attn_w_qkv, f32))
    wout1 = cc(np.asarray(L1_attn_w_out, f32))
    bout1 = np.asarray(L1_attn_b_out, f32).reshape(1, D)
    wgate = cc(np.asarray(L1_ffn_Wg, f32))
    bgv = np.asarray(L1_ffn_bg, f32).reshape(1, 8)
    W1 = np.asarray(L1_ffn_W1, f32)
    W2 = np.asarray(L1_ffn_W2, f32)
    W3 = np.asarray(L1_ffn_W3, f32)

    core_ids = list(range(NCORES))

    maps = []
    for c in range(NCORES):
        maps.append({"x": cc(x[c]), "wg_t": wg_t, "wu_t": wu_t, "wd_t": wd_t,
                     "wqkv0": wqkv0, "qkb0": qkb0, "vb0": vb0, "lw_t": lw_t,
                     "lb0": lb0, "wgate": wgate, "bg": bgv})
    r1 = run_bass_kernel_spmd(pr["l1"], maps, core_ids=core_ids)
    x4pre = [np.asarray(r1.results[c]["xo"]) for c in range(NCORES)]
    lnx4 = [np.asarray(r1.results[c]["lnx"]).view(bfd) for c in range(NCORES)]
    lg4 = [np.asarray(r1.results[c]["lg"]) for c in range(NCORES)]

    def rearr(lg_list):
        a = np.concatenate(lg_list, axis=0)
        return cc(a.reshape(TCH, 128, 8).swapaxes(0, 1).reshape(128, TCH * 8))

    def rearr_own(a):
        return cc(a.reshape(NCH, 128, 8).swapaxes(0, 1).reshape(128, NCH * 8))

    lg4_all = rearr(lg4)
    lnx4_all = cc(np.concatenate(lnx4 + [np.zeros((1, D), bfd)], axis=0))
    esels = []
    for c in range(NCORES):
        e1 = np.zeros((128, 8), f32)
        e1[:, c] = 1.0
        esels.append(e1)

    maps = [{"lga": lg4_all, "lnxa": lnx4_all, "w1": cc(W1[c]), "w2": cc(W2[c]),
             "w3": cc(W3[c]), "esel": esels[c]} for c in range(NCORES)]
    r2a = run_bass_kernel_spmd(pr["l2a"], maps, core_ids=core_ids)
    o4_all = cc(np.concatenate([np.asarray(r2a.results[c]["o"]) for c in range(NCORES)]
                               + [np.zeros((1, D), f32)], axis=0))

    maps = [{"x": x4pre[c], "lga": lg4_all, "lgown": rearr_own(lg4[c]), "oall": o4_all,
             "ownsel": esels[c], "wqkv1": wqkv1, "wout1": wout1, "bout1": bout1,
             "wgate": wgate, "bg": bgv} for c in range(NCORES)]
    r2b = run_bass_kernel_spmd(pr["l2b"], maps, core_ids=core_ids)
    x6pre = [np.asarray(r2b.results[c]["xo"]) for c in range(NCORES)]
    lnx6 = [np.asarray(r2b.results[c]["lnx"]).view(bfd) for c in range(NCORES)]
    lg6 = [np.asarray(r2b.results[c]["lg"]) for c in range(NCORES)]

    lg6_all = rearr(lg6)
    lnx6_all = cc(np.concatenate(lnx6 + [np.zeros((1, D), bfd)], axis=0))

    maps = [{"lga": lg6_all, "lnxa": lnx6_all, "w1": cc(W1[c]), "w2": cc(W2[c]),
             "w3": cc(W3[c]), "esel": esels[c]} for c in range(NCORES)]
    r3a = run_bass_kernel_spmd(pr["l2a"], maps, core_ids=core_ids)
    o6_all = cc(np.concatenate([np.asarray(r3a.results[c]["o"]) for c in range(NCORES)]
                               + [np.zeros((1, D), f32)], axis=0))

    maps = [{"x": x6pre[c], "lga": lg6_all, "lgown": rearr_own(lg6[c]), "oall": o6_all,
             "ownsel": esels[c]} for c in range(NCORES)]
    r3b = run_bass_kernel_spmd(pr["l3b"], maps, core_ids=core_ids)
    y = np.stack([np.asarray(r3b.results[c]["y"]) for c in range(NCORES)], axis=0)
    return y.astype(np.float32)
